# version 03 (final)
# baseline (speedup 1.0000x reference)
import numpy as np
import ml_dtypes
import concourse.bass as bass
import concourse.mybir as mybir
from concourse.bass import IndirectOffsetOnAxis
from concourse.bass_utils import run_bass_kernel_spmd

F32 = mybir.dt.float32
BF16 = mybir.dt.bfloat16
I32 = mybir.dt.int32
AF = mybir.ActivationFunctionType
ALU = mybir.AluOpType
AX = mybir.AxisListType

D = 2048
T = 2048
NH = 16
HDIM = 128
H0 = 5632
F1 = 8192
TG = 1024
NB = 8
NEGBIG = -10000.0

N_SUB = 6
L1FFN_PARTS = "abcdefgh"


class Sem:
    __slots__ = ("h", "n")

    def __init__(self, h):
        self.h = h
        self.n = 0


class Prog:

    def __init__(self, nc):
        self.nc = nc
        self.sems = {}
        self.groups = {}
        self.phys = {}
        self.t = {}
        self._uid = 0

    def _sem(self, name, cls):
        key = (cls, name)
        if key not in self.sems:
            grp = (cls, name.split("_")[0])
            idx = self.groups.setdefault(grp, {})
            slot = (cls, idx.setdefault(name, len(idx)))
            if slot not in self.phys:
                self.phys[slot] = Sem(
                    self.nc.alloc_semaphore(f"S{cls}{slot[1]}"))
            self.sems[key] = self.phys[slot]
        return self.sems[key]

    def sem(self, name):
        return self._sem(name, "c")

    def gsem(self, name):
        return self._sem(name, "g")

    def un(self, base):
        self._uid += 1
        return f"{base}_{self._uid}"



def _vec_tile(v):
    n = np.asarray(v).shape[0]
    return np.ascontiguousarray(np.asarray(v).reshape(n // 128, 128).T)


def host_layout(inputs):
    g = {}
    g["l0_wq"] = inputs["L0_attn_wq"]
    g["l0_wk"] = inputs["L0_attn_wk"]
    g["l0_wv"] = inputs["L0_attn_wv"]
    g["l0_wo"] = inputs["L0_attn_wo"]
    g["l0_bq"] = _vec_tile(inputs["L0_attn_bq"])
    g["l0_bk"] = _vec_tile(inputs["L0_attn_bk"])
    g["l0_bv"] = _vec_tile(inputs["L0_attn_bv"])
    g["l0_bo"] = _vec_tile(inputs["L0_attn_bo"])
    g["l0_wg"] = np.asarray(inputs["L0_ffn_wq_gate"]).T
    g["l0_wu"] = np.asarray(inputs["L0_ffn_wq_up"]).T
    g["l0_wd"] = np.asarray(inputs["L0_ffn_wq_down"]).T
    g["l0_sg"] = _vec_tile(inputs["L0_ffn_s_gate"])
    g["l0_su"] = _vec_tile(inputs["L0_ffn_s_up"])
    g["l0_sd"] = _vec_tile(inputs["L0_ffn_s_down"])
    g["l1a_wq"] = np.asarray(inputs["L1_attn_Wq"]).T
    g["l1a_wk"] = np.asarray(inputs["L1_attn_Wk"]).T
    g["l1a_wv"] = np.asarray(inputs["L1_attn_Wv"]).T
    g["l1a_wo"] = np.asarray(inputs["L1_attn_Wo"]).T
    g["l1_maskT"] = np.asarray(inputs["L1_ffn_mask"]).T
    fc = np.asarray(inputs["L1_ffn_freqs_cis"])
    g["l1_cos"] = fc[:, :, 0].T
    g["l1_sin"] = fc[:, :, 1].T
    g["l1_wr"] = _vec_tile(inputs["L1_ffn_w_router"])
    g["l1_g1"] = _vec_tile(inputs["L1_ffn_g1"])
    g["l1_g2"] = _vec_tile(inputs["L1_ffn_g2"])
    g["l1_wq"] = inputs["L1_ffn_wq"]
    g["l1_wk"] = inputs["L1_ffn_wk"]
    g["l1_wv"] = inputs["L1_ffn_wv"]
    g["l1_wo"] = inputs["L1_ffn_wo"]
    g["l1_w1"] = inputs["L1_ffn_w1"]
    g["l1_w3"] = inputs["L1_ffn_w3"]
    g["l1_w2"] = inputs["L1_ffn_w2"]
    g = {k: np.ascontiguousarray(np.asarray(v)) for k, v in g.items()}
    xs = [np.ascontiguousarray(np.asarray(inputs["L0_attn_x"][b]).T)
          for b in range(NB)]
    return g, xs


def _l1attn_tiles():
    tiles = {}
    masks = {}
    jj = np.arange(512)[None, :]
    ii = np.arange(128)[:, None]
    for qb in range(4):
        for kc in range(16):
            dlt = qb * 512 - kc * 128
            dmin, dmax = dlt - 127, dlt + 511
            if dmax < 0 or dmin >= 1024:
                tiles[(qb, kc)] = None
            elif dmin >= 0 and dmax < 1024:
                tiles[(qb, kc)] = "full"
            else:
                tiles[(qb, kc)] = dlt
                if dlt not in masks:
                    dd = jj - ii + dlt
                    ok = (dd >= 0) & (dd < 1024)
                    masks[dlt] = np.where(ok, 0.0, NEGBIG).astype(
                        ml_dtypes.bfloat16)
    return tiles, masks


def _consts():
    c = {}
    c["ident_bf"] = np.eye(128, dtype=ml_dtypes.bfloat16)
    c["ident_f"] = np.eye(128, dtype=np.float32)
    c["lstrict"] = np.triu(np.ones((128, 128), dtype=np.float32), 1)
    c["sshift"] = np.eye(128, 128, 1, dtype=np.float32)
    c["iota"] = np.arange(128, dtype=np.float32).reshape(128, 1)
    c["tval"] = (np.arange(16)[None, :] * 128
                 + np.arange(128)[:, None]).astype(np.float32)
    c["ones_f"] = np.ones((128, 1), dtype=np.float32)
    c["ones1_f"] = np.ones((1, 128), dtype=np.float32)
    c["ones_bf"] = np.ones((128, 1), dtype=ml_dtypes.bfloat16)
    c["ones1_bf"] = np.ones((1, 128), dtype=ml_dtypes.bfloat16)
    return c



def emit_ln(p, xd, xn, S, t0, want_xsum=False, want_router=False,
            xnd=None, xnt=None):
    nc = p.nc
    NC = 16
    nb = S // 512
    lx = [p.sem(f"ln_lx{j}") for j in range(4)]
    qs = p.sem("ln_sq")
    stm = p.sem("ln_stm")
    rowr = p.sem("ln_rowr")
    sqd = p.sem("ln_sqd")
    rowf = p.sem("ln_rowf")
    bcm = p.sem("ln_bcm")
    nrm = p.sem("ln_nrm")

    ones_f = p.t["ones_f"]
    ones1_f = p.t["ones1_f"]

    with (
        nc.sbuf_tensor(p.un("lnx"), [128, NC * S], F32) as xh,
        nc.sbuf_tensor(p.un("lnsq"), [128, 2 * S], F32) as sqt,
        nc.sbuf_tensor(p.un("lnrow"), [1, 6 * S], F32) as rows,
    ):
        mean = rows[:, 0:S]
        rstd = rows[:, S:2 * S]
        nmr = rows[:, 2 * S:3 * S]
        var = rows[:, 3 * S:4 * S]
        m2 = rows[:, 4 * S:5 * S]
        sd = rows[:, 5 * S:6 * S]

        with (
            nc.psum_tensor(p.un("lnst"), [1, 2 * S], F32) as st_ps,
            nc.psum_tensor(p.un("lnbc"), [128, 2 * S], F32) as bc_ps,
            nc.Block() as blk,
        ):
            b_lx = [s.n for s in lx]
            b_qs, b_stm, b_rowr, b_sqd = qs.n, stm.n, rowr.n, sqd.n
            b_rowf, b_bcm, b_nrm = rowf.n, bcm.n, nrm.n

            @blk.sync
            def _(s):
                for j in range(4):
                    src = xd.ap()[j * 512:(j + 1) * 512, t0:t0 + S]
                    src = src.rearrange("(c p) f -> p c f", p=128)
                    dst = xh[:, j * 4 * S:(j + 1) * 4 * S].rearrange(
                        "p (c f) -> p c f", c=4)
                    s.dma_start(out=dst, in_=src).then_inc(lx[j].h, 16)
                    lx[j].n += 16

            @blk.scalar
            def _(sc):
                for dc in range(NC):
                    sc.wait_ge(lx[dc // 4].h, b_lx[dc // 4] + 16)
                    if dc >= 2:
                        sc.wait_ge(stm.h, b_stm + (dc - 1))
                    sc.activation(sqt[:, (dc % 2) * S:(dc % 2 + 1) * S],
                                  xh[:, dc * S:(dc + 1) * S],
                                  AF.Square).then_inc(qs.h, 1)
                    qs.n += 1
                sc.wait_ge(rowr.h, b_rowr + 1)
                sc.activation(sd[:, :], var[:, :], AF.Sqrt).then_inc(sqd.h, 1)
                sqd.n += 1

            @blk.tensor
            def _(t):
                for dc in range(NC):
                    t.wait_ge(lx[dc // 4].h, b_lx[dc // 4] + 16)
                    t.wait_ge(qs.h, b_qs + dc + 1)
                    first, last = dc == 0, dc == NC - 1
                    ins = None
                    for b in range(nb):
                        t.matmul(st_ps[0:1, b * 512:(b + 1) * 512],
                                 ones_f[:, 0:1],
                                 xh[:, dc * S + b * 512:dc * S + (b + 1) * 512],
                                 start=first, stop=last)
                        ins = t.matmul(
                            st_ps[0:1, S + b * 512:S + (b + 1) * 512],
                            ones_f[:, 0:1],
                            sqt[:, (dc % 2) * S + b * 512:
                                (dc % 2) * S + (b + 1) * 512],
                            start=first, stop=last)
                    ins.then_inc(stm.h, 1)
                    stm.n += 1
                t.wait_ge(rowf.h, b_rowf + 1)
                ins = None
                for b in range(nb):
                    t.matmul(bc_ps[:, b * 512:(b + 1) * 512], ones1_f[:, :],
                             rstd[:, b * 512:(b + 1) * 512],
                             start=True, stop=True)
                    ins = t.matmul(bc_ps[:, S + b * 512:S + (b + 1) * 512],
                                   ones1_f[:, :], nmr[:, b * 512:(b + 1) * 512],
                                   start=True, stop=True)
                ins.then_inc(bcm.h, 1)
                bcm.n += 1

            @blk.vector
            def _(v):
                vch = p.sem("ln_vch")

                def step(ins):
                    ins.then_inc(vch.h, 1)
                    vch.n += 1
                    v.wait_ge(vch.h, vch.n)

                v.wait_ge(stm.h, b_stm + NC)
                step(v.tensor_scalar(mean[:, :], st_ps[0:1, 0:S], 1.0 / D,
                                     None, ALU.mult))
                step(v.tensor_scalar(var[:, :], st_ps[0:1, S:2 * S], 1.0 / D,
                                     None, ALU.mult))
                step(v.tensor_mul(m2[:, :], mean[:, :], mean[:, :]))
                step(v.tensor_sub(var[:, :], var[:, :], m2[:, :]))
                v.tensor_scalar(var[:, :], var[:, :], 1e-5, None,
                                ALU.add).then_inc(rowr.h, 1)
                rowr.n += 1
                v.wait_ge(sqd.h, b_sqd + 1)
                step(v.reciprocal(rstd[:, :], sd[:, :]))
                step(v.tensor_mul(nmr[:, :], mean[:, :], rstd[:, :]))
                v.tensor_scalar(nmr[:, :], nmr[:, :], -1.0, None,
                                ALU.mult).then_inc(rowf.h, 1)
                rowf.n += 1
                v.wait_ge(bcm.h, b_bcm + 1)
                v.wait_ge(qs.h, b_qs + NC)
                for dc in range(NC):
                    xs_ = xh[:, dc * S:(dc + 1) * S]
                    tmp = sqt[:, (dc % 2) * S:(dc % 2 + 1) * S]
                    v.tensor_mul(tmp, xs_, bc_ps[:, 0:S]).then_inc(nrm.h, 1)
                    nrm.n += 1
                    v.wait_ge(nrm.h, nrm.n)
                    v.tensor_add(xn[:, dc * S:(dc + 1) * S], tmp,
                                 bc_ps[:, S:2 * S]).then_inc(nrm.h, 1)
                    nrm.n += 1

    if not (want_xsum or want_router or xnd is not None or xnt is not None):
        return
    exm = p.sem("ln_exm")
    tpm = p.sem("ln_tpm")
    evc = p.sem("ln_evc")
    stw = [p.sem("ln_stw0"), p.sem("ln_stw1")]
    spw = p.gsem("ln_spw")
    erow = p.sem("ln_erow")
    NTC = S // 128

    with (
        nc.sbuf_tensor(p.un("lntr"), [128, 2 * 2048], BF16) as trow,
        nc.psum_tensor(p.un("lnex"), [1, 2 * S], F32) as ex_ps,
        nc.psum_tensor(p.un("lntp"), [128, 2 * 2048], BF16) as tp_ps,
        nc.Block() as blk,
    ):
        b_exm, b_tpm, b_evc = exm.n, tpm.n, evc.n
        b_stw = [s.n for s in stw]

        @blk.tensor
        def _(t):
            if want_xsum or want_router:
                for dc in range(NC):
                    first, last = dc == 0, dc == NC - 1
                    ins = None
                    for b in range(nb):
                        xnb = xn[:, dc * S + b * 512:dc * S + (b + 1) * 512]
                        if want_xsum:
                            ins = t.matmul(ex_ps[0:1, b * 512:(b + 1) * 512],
                                           p.t["ones_bf"][:, 0:1], xnb,
                                           start=first, stop=last)
                        if want_router:
                            ins = t.matmul(
                                ex_ps[0:1, S + b * 512:S + (b + 1) * 512],
                                p.t["wr_bf"][:, dc:dc + 1], xnb,
                                start=first, stop=last)
                    ins.then_inc(exm.h, 1)
                    exm.n += 1
            if xnt is not None:
                for tc in range(NTC):
                    if tc >= 2:
                        t.wait_ge(evc.h, b_evc + tc - 1)
                    half = (tc % 2) * 2048
                    ins = None
                    for dc in range(NC):
                        ins = t.transpose(
                            tp_ps[:, half + dc * 128:half + (dc + 1) * 128],
                            xn[:, dc * S + tc * 128:dc * S + tc * 128 + 128],
                            p.t["ident_bf"][:, :])
                    ins.then_inc(tpm.h, 1)
                    tpm.n += 1

        @blk.vector
        def _(v):
            if want_xsum:
                v.wait_ge(exm.h, b_exm + NC)
                v.tensor_scalar(p.t["xsum_row"][:, 0:S], ex_ps[0:1, 0:S],
                                -128.0, None, ALU.mult).then_inc(erow.h, 1)
                erow.n += 1
            if want_router:
                v.wait_ge(exm.h, b_exm + NC)
                v.tensor_copy(p.t["router_row"][:, 0:S],
                              ex_ps[0:1, S:2 * S]).then_inc(erow.h, 1)
                erow.n += 1
            if xnt is not None:
                for tc in range(NTC):
                    v.wait_ge(tpm.h, b_tpm + tc + 1)
                    if tc >= 2:
                        v.wait_ge(stw[tc % 2].h, b_stw[tc % 2] + 16 * (tc // 2))
                    half = (tc % 2) * 2048
                    v.tensor_copy(trow[:, half:half + 2048],
                                  tp_ps[:, half:half + 2048]).then_inc(evc.h, 1)
                    evc.n += 1

        @blk.sync
        def _(s):
            if xnt is not None:
                for tc in range(NTC):
                    s.wait_ge(evc.h, b_evc + tc + 1)
                    ins = s.dma_start(
                        out=xnt.ap()[t0 + tc * 128:t0 + tc * 128 + 128, :],
                        in_=trow[:, (tc % 2) * 2048:(tc % 2) * 2048 + 2048])
                    ins.then_inc(stw[tc % 2].h, 16)
                    stw[tc % 2].n += 16
                s.wait_ge(stw[0].h, stw[0].n)
                s.wait_ge(stw[1].h, stw[1].n)

        @blk.gpsimd
        def _(gp):
            if xnd is not None:
                dst = xnd.ap()[:, t0:t0 + S].rearrange("(c p) f -> p c f",
                                                       p=128)
                src = xn[:, 0:NC * S].rearrange("p (c f) -> p c f", c=NC)
                gp.dma_start(out=dst, in_=src).then_inc(spw.h, 16)
                spw.n += 16
                gp.wait_ge(spw.h, spw.n)


def emit_l0ffn_half(p, xd, th, g):
    nc = p.nc
    S = 1024
    t0 = th * S
    nb = S // 512
    NKC = 16
    NHC = 44

    with (
        nc.sbuf_tensor(p.un("f0xn"), [128, 16 * S], BF16) as xn,
        nc.sbuf_tensor(p.un("f0xs"), [1, S], F32) as xsum_row,
    ):
        p.t["xsum_row"] = xsum_row
        emit_ln(p, xd, xn, S, t0, want_xsum=True)
        _emit_l0ffn_mats(p, xd, th, g, xn, xsum_row)


def _emit_l0ffn_mats(p, xd, th, g, xn, xsum_row):
    nc = p.nc
    S = 1024
    t0 = th * S
    nb = S // 512
    NKC = 16
    NHC = 44

    with (
        nc.sbuf_tensor(p.un("f0gu"), [128, NHC * S], BF16) as gu,
        nc.sbuf_tensor(p.un("f0gr"), [1, S], F32) as ngu_row,
    ):
        ldg = [p.gsem("f0_ldg0"), p.gsem("f0_ldg1")]
        ldu = [p.gsem("f0_ldu0"), p.gsem("f0_ldu1")]
        mmg = p.sem("f0_mmg")
        evg = p.sem("f0_evg")
        evu = p.sem("f0_evu")
        evp = p.sem("f0_evp")
        gsm = p.sem("f0_gsm")
        grow = p.sem("f0_grow")

        with (
            nc.sbuf_tensor(p.un("f0w"), [128, 2 * 32 * 128], BF16) as wt,
            nc.sbuf_tensor(p.un("f0g"), [128, S], BF16) as gtmp,
            nc.sbuf_tensor(p.un("f0u"), [128, S], BF16) as utmp,
            nc.psum_tensor(p.un("f0pg"), [128, S], F32) as gps,
            nc.psum_tensor(p.un("f0pu"), [128, S], F32) as ups,
            nc.psum_tensor(p.un("f0ps"), [1, S], F32) as gsum_ps,
            nc.Block() as blk,
        ):
            b_ldg = [s.n for s in ldg]
            b_ldu = [s.n for s in ldu]
            b_mmg, b_evg, b_evu, b_evp, b_gsm = (mmg.n, evg.n, evu.n,
                                                 evp.n, gsm.n)

            @blk.gpsimd
            def _(gp):
                for hc in range(NHC):
                    j = hc % 2
                    if hc >= 2:
                        gp.wait_ge(mmg.h, b_mmg + hc - 1)
                    half = j * 32 * 128
                    src = g["l0_wg_d"].ap()[:, hc * 128:(hc + 1) * 128]
                    src = src.rearrange("(c p) h -> p c h", p=128)
                    dst = wt[:, half:half + 16 * 128].rearrange(
                        "p (c h) -> p c h", c=16)
                    gp.dma_start(out=dst, in_=src).then_inc(ldg[j].h, 16)
                    ldg[j].n += 16
                    src2 = g["l0_wu_d"].ap()[:, hc * 128:(hc + 1) * 128]
                    src2 = src2.rearrange("(c p) h -> p c h", p=128)
                    dst2 = wt[:, half + 16 * 128:half + 32 * 128].rearrange(
                        "p (c h) -> p c h", c=16)
                    gp.dma_start(out=dst2, in_=src2).then_inc(ldu[j].h, 16)
                    ldu[j].n += 16

            @blk.tensor
            def _(t):
                for hc in range(NHC):
                    j = hc % 2
                    half = j * 32 * 128
                    t.wait_ge(ldg[j].h, b_ldg[j] + 16 * (hc // 2 + 1))
                    t.wait_ge(ldu[j].h, b_ldu[j] + 16 * (hc // 2 + 1))
                    if hc >= 2:
                        t.wait_ge(evg.h, b_evg + hc - 1)
                        t.wait_ge(evu.h, b_evu + hc - 1)
                    for kc in range(NKC):
                        first = kc == 0
                        for b in range(nb):
                            t.matmul(gps[:, b * 512:(b + 1) * 512],
                                     wt[:, half + kc * 128:half + kc * 128 + 128],
                                     xn[:, kc * S + b * 512:kc * S + (b + 1) * 512],
                                     start=first, stop=False)
                            t.matmul(ups[:, b * 512:(b + 1) * 512],
                                     wt[:, half + 16 * 128 + kc * 128:
                                        half + 16 * 128 + kc * 128 + 128],
                                     xn[:, kc * S + b * 512:kc * S + (b + 1) * 512],
                                     start=first, stop=False)
                    ins = None
                    for b in range(nb):
                        t.matmul(gps[:, b * 512:(b + 1) * 512],
                                 p.t["ones1_f"][:, :],
                                 xsum_row[:, b * 512:(b + 1) * 512],
                                 start=False, stop=True)
                        ins = t.matmul(ups[:, b * 512:(b + 1) * 512],
                                       p.t["ones1_f"][:, :],
                                       xsum_row[:, b * 512:(b + 1) * 512],
                                       start=False, stop=True)
                    ins.then_inc(mmg.h, 1)
                    mmg.n += 1
                    t.wait_ge(evp.h, b_evp + hc + 1)
                    first, last = hc == 0, hc == NHC - 1
                    ins = None
                    for b in range(nb):
                        ins = t.matmul(gsum_ps[0:1, b * 512:(b + 1) * 512],
                                       p.t["ones_bf"][:, 0:1],
                                       gu[:, hc * S + b * 512:
                                          hc * S + (b + 1) * 512],
                                       start=first, stop=last)
                    ins.then_inc(gsm.h, 1)
                    gsm.n += 1

            @blk.scalar
            def _(sc):
                for hc in range(NHC):
                    sc.wait_ge(mmg.h, b_mmg + hc + 1)
                    if hc >= 1:
                        sc.wait_ge(evp.h, b_evp + hc)
                    sc.activation(gtmp[:, :], gps[:, :], AF.Silu,
                                  scale=g["sg_sb"][:, hc:hc + 1]).then_inc(
                        evg.h, 1)
                    evg.n += 1

            @blk.vector
            def _(v):
                for hc in range(NHC):
                    v.wait_ge(mmg.h, b_mmg + hc + 1)
                    if hc >= 1:
                        v.wait_ge(evp.h, b_evp + hc)
                    v.tensor_scalar(utmp[:, :], ups[:, :],
                                    g["su_sb"][:, hc:hc + 1], None,
                                    ALU.mult).then_inc(evu.h, 1)
                    evu.n += 1
                    v.wait_ge(evg.h, b_evg + hc + 1)
                    v.wait_ge(evu.h, evu.n)
                    v.tensor_mul(gu[:, hc * S:(hc + 1) * S], gtmp[:, :],
                                 utmp[:, :]).then_inc(evp.h, 1)
                    evp.n += 1
                v.wait_ge(gsm.h, b_gsm + NHC)
                v.tensor_scalar(ngu_row[:, :], gsum_ps[0:1, :], -128.0, None,
                                ALU.mult).then_inc(grow.h, 1)
                grow.n += 1

        ldd = [p.gsem("f0_ldd0"), p.gsem("f0_ldd1")]
        lxo = [p.sem("f0_lxo0"), p.sem("f0_lxo1")]
        mmd = p.sem("f0_mmd")
        evd = p.sem("f0_evd")
        std = [p.sem("f0_std0"), p.sem("f0_std1")]

        with (
            nc.sbuf_tensor(p.un("f0wd"), [128, 2 * NHC * 128], BF16) as wd,
            nc.sbuf_tensor(p.un("f0xo"), [128, 2 * S], F32) as xold,
            nc.sbuf_tensor(p.un("f0xw"), [128, 2 * S], F32) as xnew,
            nc.psum_tensor(p.un("f0pd"), [128, 2 * S], F32) as dps,
            nc.Block() as blk,
        ):
            b_ldd = [s.n for s in ldd]
            b_lxo = [s.n for s in lxo]
            b_mmd, b_evd = mmd.n, evd.n
            b_std = [s.n for s in std]
            grow_target = grow.n

            @blk.gpsimd
            def _(gp):
                for oc in range(16):
                    j = oc % 2
                    if oc >= 2:
                        gp.wait_ge(mmd.h, b_mmd + oc - 1)
                    src = g["l0_wd_d"].ap()[:, oc * 128:(oc + 1) * 128]
                    src = src.rearrange("(c p) h -> p c h", p=128)
                    dst = wd[:, j * NHC * 128:(j + 1) * NHC * 128]
                    dst = dst.rearrange("p (c h) -> p c h", c=NHC)
                    gp.dma_start(out=dst, in_=src).then_inc(ldd[j].h, 16)
                    ldd[j].n += 16

            @blk.sync
            def _(s):
                for oc in range(16):
                    j = oc % 2
                    if oc >= 2:
                        s.wait_ge(evd.h, b_evd + oc - 1)
                    src = xd.ap()[oc * 128:(oc + 1) * 128, t0:t0 + S]
                    s.dma_start(out=xold[:, j * S:(j + 1) * S],
                                in_=src).then_inc(lxo[j].h, 16)
                    lxo[j].n += 16

            @blk.scalar
            def _(sc):
                for oc in range(16):
                    j = oc % 2
                    sc.wait_ge(evd.h, b_evd + oc + 1)
                    ins = sc.dma_start(
                        out=xd.ap()[oc * 128:(oc + 1) * 128, t0:t0 + S],
                        in_=xnew[:, j * S:(j + 1) * S])
                    ins.then_inc(std[j].h, 16)
                    std[j].n += 16
                sc.wait_ge(std[0].h, std[0].n)
                sc.wait_ge(std[1].h, std[1].n)

            @blk.tensor
            def _(t):
                t.wait_ge(grow.h, grow_target)
                for oc in range(16):
                    j = oc % 2
                    t.wait_ge(ldd[j].h, b_ldd[j] + 16 * (oc // 2 + 1))
                    if oc >= 2:
                        t.wait_ge(evd.h, b_evd + oc - 1)
                    for kc in range(NHC):
                        for b in range(nb):
                            t.matmul(
                                dps[:, j * S + b * 512:j * S + (b + 1) * 512],
                                wd[:, j * NHC * 128 + kc * 128:
                                   j * NHC * 128 + kc * 128 + 128],
                                gu[:, kc * S + b * 512:kc * S + (b + 1) * 512],
                                start=(kc == 0), stop=False)
                    ins = None
                    for b in range(nb):
                        ins = t.matmul(
                            dps[:, j * S + b * 512:j * S + (b + 1) * 512],
                            p.t["ones1_f"][:, :],
                            ngu_row[:, b * 512:(b + 1) * 512],
                            start=False, stop=True)
                    ins.then_inc(mmd.h, 1)
                    mmd.n += 1

            @blk.vector
            def _(v):
                for oc in range(16):
                    j = oc % 2
                    v.wait_ge(mmd.h, b_mmd + oc + 1)
                    v.wait_ge(lxo[j].h, b_lxo[j] + 16 * (oc // 2 + 1))
                    if oc >= 2:
                        v.wait_ge(std[j].h, b_std[j] + 16 * (oc // 2))
                    v.scalar_tensor_tensor(
                        xnew[:, j * S:(j + 1) * S],
                        dps[:, j * S:(j + 1) * S],
                        g["sdh_sb"][:, oc:oc + 1],
                        xold[:, j * S:(j + 1) * S],
                        ALU.mult, ALU.add).then_inc(evd.h, 1)
                    evd.n += 1


def emit_proj(p, g, xn, S, t0, w_dram, out_dram=None, bias_sb=None,
              norm_rows=None, vtile=None, wT=False):
    nc = p.nc
    nb = S // 512
    NOC = 16
    NKC = 16
    lw = [p.gsem("pj_lw0"), p.gsem("pj_lw1")]
    mm = p.sem("pj_mm")
    ev = p.sem("pj_ev")
    sq = p.sem("pj_sq")
    qn = p.sem("pj_qn")
    tp = p.sem("pj_tp")
    tev = p.sem("pj_tev")
    st = [p.sem("pj_st0"), p.sem("pj_st1")]
    rw = p.sem("pj_rw")
    sqn = p.sem("pj_sqn")

    with (
        nc.sbuf_tensor(p.un("pjw"), [128, 2 * 16 * 128], BF16) as wt,
        nc.sbuf_tensor(p.un("pjo"), [128, 2 * S], BF16) as osb,
        nc.sbuf_tensor(p.un("pjsq"), [128, S], F32) as sqs,
        nc.sbuf_tensor(p.un("pjqn"), [1, NOC * S], BF16) as qnrow,
        nc.sbuf_tensor(p.un("pjvt"), [128, 2 * (S // 128) * 128], BF16) as vts,
        nc.psum_tensor(p.un("pjp"), [128, 2 * S], F32) as pps,
        nc.psum_tensor(p.un("pjqn"), [1, S], F32) as qn_ps,
        nc.psum_tensor(p.un("pjtp"), [128, S], BF16) as tp_ps,
        nc.Block() as blk,
    ):
        b_lw = [s.n for s in lw]
        b_mm, b_ev, b_sq, b_qn = mm.n, ev.n, sq.n, qn.n
        b_rw = rw.n
        b_tp, b_tev = tp.n, tev.n
        b_st = [s.n for s in st]
        NTC = S // 128

        @blk.gpsimd
        def _(gp):
            for oc in range(NOC):
                j = oc % 2
                if oc >= 2:
                    gp.wait_ge(mm.h, b_mm + oc - 1)
                src = w_dram.ap()[:, oc * 128:(oc + 1) * 128]
                src = src.rearrange("(c p) h -> p c h", p=128)
                dst = wt[:, j * 2048:(j + 1) * 2048].rearrange(
                    "p (c h) -> p c h", c=16)
                gp.dma_start(out=dst, in_=src).then_inc(lw[j].h, 16)
                lw[j].n += 16

        @blk.tensor
        def _(t):
            for oc in range(NOC):
                j = oc % 2
                t.wait_ge(lw[j].h, b_lw[j] + 16 * (oc // 2 + 1))
                if oc >= 2:
                    t.wait_ge(ev.h, b_ev + oc - 1)
                    if norm_rows is not None:
                        t.wait_ge(sq.h, b_sq + oc - 1)
                ins = None
                for kc in range(NKC):
                    for b in range(nb):
                        ins = t.matmul(
                            pps[:, j * S + b * 512:j * S + (b + 1) * 512],
                            wt[:, j * 2048 + kc * 128:j * 2048 + kc * 128 + 128],
                            xn[:, kc * S + b * 512:kc * S + (b + 1) * 512],
                            start=(kc == 0), stop=(kc == NKC - 1))
                ins.then_inc(mm.h, 1)
                mm.n += 1
                if norm_rows is not None:
                    t.wait_ge(sq.h, b_sq + oc + 1)
                    if oc >= 1:
                        t.wait_ge(rw.h, b_rw + oc)
                    ins = None
                    for b in range(nb):
                        ins = t.matmul(qn_ps[0:1, b * 512:(b + 1) * 512],
                                       p.t["ones_f"][:, 0:1],
                                       sqs[:, b * 512:(b + 1) * 512],
                                       start=True, stop=True)
                    ins.then_inc(qn.h, 1)
                    qn.n += 1
                if vtile is not None:
                    t.wait_ge(ev.h, b_ev + oc + 1)
                    if oc >= 1:
                        t.wait_ge(tev.h, b_tev + oc)
                    ins = None
                    for tc in range(NTC):
                        ins = t.transpose(
                            tp_ps[:, tc * 128:(tc + 1) * 128],
                            osb[:, j * S + tc * 128:j * S + tc * 128 + 128],
                            p.t["ident_bf"][:, :])
                    ins.then_inc(tp.h, 1)
                    tp.n += 1

        @blk.scalar
        def _(sc):
            for oc in range(NOC):
                j = oc % 2
                sc.wait_ge(mm.h, b_mm + oc + 1)
                if oc >= 2:
                    sc.wait_ge(st[j].h, b_st[j] + 16 * (oc // 2))
                if bias_sb is not None:
                    ins = sc.add(osb[:, j * S:(j + 1) * S],
                                 pps[:, j * S:(j + 1) * S],
                                 bias_sb[:, oc:oc + 1])
                else:
                    ins = sc.copy(osb[:, j * S:(j + 1) * S],
                                  pps[:, j * S:(j + 1) * S])
                ins.then_inc(ev.h, 1)
                ev.n += 1
                if norm_rows is not None:
                    sc.activation(sqs[:, :], pps[:, j * S:(j + 1) * S],
                                  AF.Square).then_inc(sq.h, 1)
                    sq.n += 1
            if norm_rows is not None:
                sc.wait_ge(rw.h, rw.n + NOC)
                sc.activation(qnrow[:, :], qnrow[:, :],
                              AF.Sqrt).then_inc(sqn.h, 1)
                sqn.n += 1

        @blk.vector
        def _(v):
            if norm_rows is not None:
                for oc in range(NOC):
                    v.wait_ge(qn.h, b_qn + oc + 1)
                    v.tensor_copy(qnrow[:, oc * S:(oc + 1) * S],
                                  qn_ps[0:1, 0:S]).then_inc(rw.h, 1)
                    rw.n += 1
            if vtile is not None:
                for oc in range(NOC):
                    v.wait_ge(tp.h, b_tp + oc + 1)
                    if oc >= 2:
                        v.wait_ge(st[oc % 2].h, b_st[oc % 2] + 16 * (oc // 2))
                    v.tensor_copy(vts[:, (oc % 2) * S:(oc % 2 + 1) * S],
                                  tp_ps[:, 0:S]).then_inc(tev.h, 1)
                    tev.n += 1

        @blk.sync
        def _(s):
            if vtile is None:
                for oc in range(NOC):
                    j = oc % 2
                    s.wait_ge(ev.h, b_ev + oc + 1)
                    ins = s.dma_start(
                        out=out_dram.ap()[oc * 128:(oc + 1) * 128, t0:t0 + S],
                        in_=osb[:, j * S:(j + 1) * S])
                    ins.then_inc(st[j].h, 16)
                    st[j].n += 16
            else:
                th = t0 // S
                for oc in range(NOC):
                    j = oc % 2
                    s.wait_ge(tev.h, b_tev + oc + 1)
                    dst = vtile.ap()[oc, th * NTC:(th + 1) * NTC, :, :]
                    dst = dst.rearrange("c p hd -> p c hd")
                    src = vts[:, j * S:(j + 1) * S].rearrange(
                        "p (c hd) -> p c hd", c=NTC)
                    ins = s.dma_start(out=dst, in_=src)
                    ins.then_inc(st[j].h, 16)
                    st[j].n += 16
            if norm_rows is not None:
                s.wait_ge(sqn.h, sqn.n)
                dst = norm_rows.ap()[:, t0:t0 + S].rearrange(
                    "(o h) f -> o h f", o=1)
                src = qnrow[:, :].rearrange("p (h f) -> p h f", h=16)
                s.dma_start(out=dst, in_=src).then_inc(st[0].h, 16)
                st[0].n += 16
            s.wait_ge(st[0].h, st[0].n)
            s.wait_ge(st[1].h, st[1].n)


def emit_rms(p, src_sb, out_sb, gsc_sb, S):
    nc = p.nc
    NC = 16
    nb = S // 512
    qs = p.sem("rm_sq")
    stm = p.sem("rm_stm")
    rowr = p.sem("rm_rowr")
    sqd = p.sem("rm_sqd")
    rowf = p.sem("rm_rowf")
    bcm = p.sem("rm_bcm")
    nrm = p.sem("rm_nrm")

    with (
        nc.sbuf_tensor(p.un("rmsq"), [128, 2 * S], F32) as sqt,
        nc.sbuf_tensor(p.un("rmrow"), [1, 3 * S], F32) as rows,
        nc.psum_tensor(p.un("rmst"), [1, S], F32) as st_ps,
        nc.psum_tensor(p.un("rmbc"), [128, S], F32) as bc_ps,
        nc.Block() as blk,
    ):
        b_qs, b_stm, b_rowr, b_sqd = qs.n, stm.n, rowr.n, sqd.n
        b_rowf, b_bcm, b_nrm = rowf.n, bcm.n, nrm.n
        ms = rows[:, 0:S]
        sd = rows[:, S:2 * S]
        rstd = rows[:, 2 * S:3 * S]

        @blk.scalar
        def _(sc):
            for dc in range(NC):
                if dc >= 2:
                    sc.wait_ge(stm.h, b_stm + (dc - 1))
                sc.activation(sqt[:, (dc % 2) * S:(dc % 2 + 1) * S],
                              src_sb[:, dc * S:(dc + 1) * S],
                              AF.Square).then_inc(qs.h, 1)
                qs.n += 1
            sc.wait_ge(rowr.h, b_rowr + 1)
            sc.activation(sd[:, :], ms[:, :], AF.Sqrt).then_inc(sqd.h, 1)
            sqd.n += 1

        @blk.tensor
        def _(t):
            for dc in range(NC):
                t.wait_ge(qs.h, b_qs + dc + 1)
                first, last = dc == 0, dc == NC - 1
                ins = None
                for b in range(nb):
                    ins = t.matmul(
                        st_ps[0:1, b * 512:(b + 1) * 512],
                        p.t["ones_f"][:, 0:1],
                        sqt[:, (dc % 2) * S + b * 512:
                            (dc % 2) * S + (b + 1) * 512],
                        start=first, stop=last)
                ins.then_inc(stm.h, 1)
                stm.n += 1
            t.wait_ge(rowf.h, b_rowf + 1)
            ins = None
            for b in range(nb):
                ins = t.matmul(bc_ps[:, b * 512:(b + 1) * 512],
                               p.t["ones1_f"][:, :],
                               rstd[:, b * 512:(b + 1) * 512],
                               start=True, stop=True)
            ins.then_inc(bcm.h, 1)
            bcm.n += 1

        @blk.vector
        def _(v):
            vch = p.sem("rm_vch")

            def stp(ins):
                ins.then_inc(vch.h, 1)
                vch.n += 1
                v.wait_ge(vch.h, vch.n)

            v.wait_ge(stm.h, b_stm + NC)
            stp(v.tensor_scalar(ms[:, :], st_ps[0:1, 0:S], 1.0 / D, None,
                                ALU.mult))
            v.tensor_scalar(ms[:, :], ms[:, :], 1e-6, None,
                            ALU.add).then_inc(rowr.h, 1)
            rowr.n += 1
            v.wait_ge(sqd.h, b_sqd + 1)
            v.reciprocal(rstd[:, :], sd[:, :]).then_inc(rowf.h, 1)
            rowf.n += 1
            v.wait_ge(bcm.h, b_bcm + 1)
            v.wait_ge(qs.h, b_qs + NC)
            for dc in range(NC):
                v.scalar_tensor_tensor(
                    out_sb[:, dc * S:(dc + 1) * S],
                    src_sb[:, dc * S:(dc + 1) * S],
                    gsc_sb[:, dc:dc + 1],
                    bc_ps[:, 0:S], ALU.mult, ALU.mult).then_inc(nrm.h, 1)
                nrm.n += 1


def emit_swiglu(p, g, h2d, resid_sb, w1_d, w3_d, w2_d, S=1024):
    nc = p.nc
    NKC, NHC = 16, 64
    l1s = [p.gsem("sw_l10"), p.gsem("sw_l11")]
    l3s = [p.gsem("sw_l30"), p.gsem("sw_l31")]
    lh2 = p.sem("sw_lh2")
    mmg = p.sem("sw_mmg")
    evg = p.sem("sw_evg")
    evu = p.sem("sw_evu")
    evp = p.sem("sw_evp")
    l2s = p.gsem("sw_l2")
    mmd = p.sem("sw_mmd")
    evd = p.sem("sw_evd")

    for ts in range(2):
        tt0 = ts * 512
        with (
            nc.sbuf_tensor(p.un("swgu"), [128, NHC * 512], BF16) as gu,
            nc.sbuf_tensor(p.un("swh2"), [128, 16 * 512], BF16) as h2s,
            nc.sbuf_tensor(p.un("sww"), [128, 2 * 2 * 16 * 128], BF16) as wt,
            nc.sbuf_tensor(p.un("swg"), [128, 512], BF16) as gtmp,
            nc.sbuf_tensor(p.un("swu"), [128, 512], BF16) as utmp,
            nc.psum_tensor(p.un("swpg"), [128, 512], F32) as gps,
            nc.psum_tensor(p.un("swpu"), [128, 512], F32) as ups,
            nc.Block() as blk,
        ):
            b_l1 = [s.n for s in l1s]
            b_l3 = [s.n for s in l3s]
            b_lh2 = lh2.n
            b_mmg, b_evg, b_evu, b_evp = mmg.n, evg.n, evu.n, evp.n

            @blk.sync
            def _(s):
                src = h2d.ap()[:, tt0:tt0 + 512].rearrange(
                    "(c p) f -> p c f", p=128)
                dst = h2s[:, :].rearrange("p (c f) -> p c f", c=16)
                s.dma_start(out=dst, in_=src).then_inc(lh2.h, 16)
                lh2.n += 16

            @blk.gpsimd
            def _(gp):
                for hc in range(NHC):
                    j = hc % 2
                    if hc >= 2:
                        gp.wait_ge(mmg.h, b_mmg + hc - 1)
                    half = j * 2 * 2048
                    src = w1_d.ap()[:, hc * 128:(hc + 1) * 128]
                    src = src.rearrange("(c p) h -> p c h", p=128)
                    dst = wt[:, half:half + 2048].rearrange(
                        "p (c h) -> p c h", c=16)
                    gp.dma_start(out=dst, in_=src).then_inc(l1s[j].h, 16)
                    l1s[j].n += 16
                    src = w3_d.ap()[:, hc * 128:(hc + 1) * 128]
                    src = src.rearrange("(c p) h -> p c h", p=128)
                    dst = wt[:, half + 2048:half + 4096].rearrange(
                        "p (c h) -> p c h", c=16)
                    gp.dma_start(out=dst, in_=src).then_inc(l3s[j].h, 16)
                    l3s[j].n += 16

            @blk.tensor
            def _(t):
                t.wait_ge(lh2.h, b_lh2 + 16)
                for hc in range(NHC):
                    j = hc % 2
                    half = j * 2 * 2048
                    t.wait_ge(l1s[j].h, b_l1[j] + 16 * (hc // 2 + 1))
                    t.wait_ge(l3s[j].h, b_l3[j] + 16 * (hc // 2 + 1))
                    if hc >= 1:
                        t.wait_ge(evg.h, b_evg + hc)
                        t.wait_ge(evu.h, b_evu + hc)
                    ins1 = ins3 = None
                    for kc in range(NKC):
                        first, last = kc == 0, kc == NKC - 1
                        ins1 = t.matmul(
                            gps[:, :],
                            wt[:, half + kc * 128:half + kc * 128 + 128],
                            h2s[:, kc * 512:(kc + 1) * 512],
                            start=first, stop=last)
                        ins3 = t.matmul(
                            ups[:, :],
                            wt[:, half + 2048 + kc * 128:
                               half + 2048 + kc * 128 + 128],
                            h2s[:, kc * 512:(kc + 1) * 512],
                            start=first, stop=last)
                    ins3.then_inc(mmg.h, 1)
                    mmg.n += 1

            @blk.scalar
            def _(sc):
                for hc in range(NHC):
                    sc.wait_ge(mmg.h, b_mmg + hc + 1)
                    if hc >= 1:
                        sc.wait_ge(evp.h, b_evp + hc)
                    sc.activation(gtmp[:, :], gps[:, :],
                                  AF.Silu).then_inc(evg.h, 1)
                    evg.n += 1

            @blk.vector
            def _(v):
                for hc in range(NHC):
                    v.wait_ge(mmg.h, b_mmg + hc + 1)
                    if hc >= 1:
                        v.wait_ge(evp.h, b_evp + hc)
                    v.tensor_copy(utmp[:, :], ups[:, :]).then_inc(evu.h, 1)
                    evu.n += 1
                    v.wait_ge(evg.h, b_evg + hc + 1)
                    v.wait_ge(evu.h, evu.n)
                    v.tensor_mul(gu[:, hc * 512:(hc + 1) * 512], gtmp[:, :],
                                 utmp[:, :]).then_inc(evp.h, 1)
                    evp.n += 1

            with (
                nc.sbuf_tensor(p.un("sww2"), [128, NHC * 128], BF16) as wd,
                nc.psum_tensor(p.un("swpd"), [128, 2 * 512], F32) as dps,
            ):
                b_l2, b_mmd, b_evd = l2s.n, mmd.n, evd.n

                @blk.gpsimd
                def _(s):
                    for oc in range(16):
                        if oc >= 1:
                            s.wait_ge(mmd.h, b_mmd + oc)
                        src = w2_d.ap()[:, oc * 128:(oc + 1) * 128]
                        src = src.rearrange("(c p) h -> p c h", p=128)
                        dst = wd[:, :].rearrange("p (c h) -> p c h", c=NHC)
                        s.dma_start(out=dst, in_=src).then_inc(l2s.h, 16)
                        l2s.n += 16

                @blk.tensor
                def _(t):
                    t.wait_ge(evp.h, b_evp + NHC)
                    for oc in range(16):
                        j = oc % 2
                        t.wait_ge(l2s.h, b_l2 + 16 * (oc + 1))
                        if oc >= 2:
                            t.wait_ge(evd.h, b_evd + oc - 1)
                        ins = None
                        for kc in range(NHC):
                            ins = t.matmul(
                                dps[:, j * 512:(j + 1) * 512],
                                wd[:, kc * 128:(kc + 1) * 128],
                                gu[:, kc * 512:(kc + 1) * 512],
                                start=(kc == 0), stop=(kc == NHC - 1))
                        ins.then_inc(mmd.h, 1)
                        mmd.n += 1

                @blk.vector
                def _(v):
                    for oc in range(16):
                        j = oc % 2
                        v.wait_ge(mmd.h, b_mmd + oc + 1)
                        dst = resid_sb[:, oc * S + tt0:oc * S + tt0 + 512]
                        v.tensor_add(dst, dps[:, j * 512:(j + 1) * 512],
                                     dst).then_inc(evd.h, 1)
                        evd.n += 1


def emit_l1ffn(p, xd, g):
    nc = p.nc
    S = 1024
    xnd, xnt = p.t["xnd"], p.t["xnt"]
    tr_d, yrow = p.t["tr_d"], p.t["yrow"]
    qd, kd, attnd, vd2 = p.t["qd"], p.t["kd"], p.t["attnd"], p.t["vd2"]
    qn_d, kn_d = p.t["qn_d"], p.t["kn_d"]
    BIGF = 65536.0

    with (
        nc.sbuf_tensor(p.un("l1rr"), [1, 2048], F32) as rr,
        nc.sbuf_tensor(p.un("l1rows"), [1, 64], F32) as tkr,
        nc.sbuf_tensor(p.un("l1rb"), [128, 16], F32) as rb,
        nc.sbuf_tensor(p.un("l1mb"), [128, 16], F32) as m_b,
        nc.sbuf_tensor(p.un("l1pos"), [128, 16], F32) as pos_f,
        nc.sbuf_tensor(p.un("l1posi"), [128, 16], I32) as pos_i,
        nc.sbuf_tensor(p.un("l1idx"), [128, 8], I32) as idx_i,
        nc.sbuf_tensor(p.un("l1w2"), [128, 8], F32) as w2sb,
        nc.sbuf_tensor(p.un("l1small"), [128, 8 * 4], F32) as smalls,
    ):
        if "a" in L1FFN_PARTS:
            for th in range(2):
                with nc.sbuf_tensor(p.un("l1xn"), [128, 16 * S], BF16) as xn:
                    p.t["router_row"] = rr[:, th * S:(th + 1) * S]
                    emit_ln(p, xd, xn, S, th * S, want_router=True,
                            xnd=xnd, xnt=xnt)

        p.t["rb_sb"] = rb
        p.t["mb_sb"] = m_b
        p.t["posf_sb"] = pos_f
        p.t["tkr_sb"] = tkr
        if "b" in L1FFN_PARTS:
            _emit_topk(p, g, rr, tkr, rb, m_b, pos_f, pos_i, BIGF)
        _emit_l1ffn_inner(p, g, xd, xnd, xnt, tr_d, yrow, pos_i, idx_i,
                          w2sb, smalls, S, BIGF)


def _emit_l1ffn_inner(p, g, xd, xnd, xnt, tr_d, yrow, pos_i, idx_i,
                      w2sb, smalls, S, BIGF):
    nc = p.nc
    qd, kd, attnd, vd2 = p.t["qd"], p.t["kd"], p.t["attnd"], p.t["vd2"]
    qn_d, kn_d = p.t["qn_d"], p.t["kn_d"]
    with nc.sbuf_tensor(p.un("l1xg"), [128, 16 * S], F32) as xg:
        if "c" in L1FFN_PARTS:
            _emit_gather(p, g, tr_d, pos_i, idx_i, w2sb, smalls, xg, xnt,
                         BIGF)

        if "d" not in L1FFN_PARTS:
            return
        with nc.sbuf_tensor(p.un("l1h1"), [128, 16 * S], BF16) as h1:
            emit_rms(p, xg, h1, g["g1_sb"], S)
            emit_proj(p, g, h1, S, 0, g["l1_wq_d"], out_dram=qd,
                      norm_rows=qn_d)
            emit_proj(p, g, h1, S, 0, g["l1_wk_d"], out_dram=kd,
                      norm_rows=kn_d)
            emit_proj(p, g, h1, S, 0, g["l1_wv_d"], vtile=vd2)

        if "e" not in L1FFN_PARTS:
            return
        with nc.sbuf_tensor(p.un("l1msk"), [128, 8 * 1024], BF16) as msk:
            mm_ap = _emit_maskprep(p, g, msk, smalls)
            wt_map = {}
            for qb in range(2):
                for kc in range(8):
                    wt_map[(qb, kc)] = msk[:, kc * 1024 + qb * 512:
                                           kc * 1024 + qb * 512 + 512]
            emit_attn_sT(p, g, qd, kd, vd2, attnd,
                         scale=float(1.0 / np.sqrt(128.0)),
                         use_norm_shift=True, win_tiles=wt_map,
                         Tq=S, Tk=S, qn_d=qn_d, kn_d=kn_d, rope=True,
                         extra_neg=mm_ap)
            emit_phaseC(p, g, attnd, g["l1_wo_d"], None, 0, S=S,
                        resid_sb=xg)

        if "f" not in L1FFN_PARTS:
            return
        with nc.sbuf_tensor(p.un("l1h2"), [128, 16 * S], BF16) as h2:
            emit_rms(p, xg, h2, g["g2_sb"], S)
            h2w = p.gsem("sw_h2w")
            with nc.Block() as blk:
                @blk.gpsimd
                def _(gp):
                    dst = attnd.ap()[:, 0:S].rearrange("(c p) f -> p c f",
                                                       p=128)
                    src = h2[:, :].rearrange("p (c f) -> p c f", c=16)
                    gp.dma_start(out=dst, in_=src).then_inc(h2w.h, 16)
                    h2w.n += 16
                    gp.wait_ge(h2w.h, h2w.n)
        emit_swiglu(p, g, attnd, xg, g["l1_w1_d"], g["l1_w3_d"],
                    g["l1_w2_d"], S)

        if "g" not in L1FFN_PARTS:
            return
        _emit_yout(p, g, xg, w2sb, yrow, S)

        if "h" not in L1FFN_PARTS:
            return
        _emit_finaladd(p, g, xd, xnd, yrow, pos_i)


def _emit_topk(p, g, rr, tkr, rb, m_b, pos_f, pos_i, BIGF):
    nc = p.nc
    tk = p.sem("tk_a")
    tm = p.sem("tk_m")
    ROUNDS = 8

    lo = tkr[:, 0:1]
    hi = tkr[:, 1:2]
    lohw = tkr[:, 2:4]
    wv = tkr[:, 3:4]
    cand = None

    with (
        nc.sbuf_tensor(p.un("tkc"), [128, 8], F32) as csb,
        nc.sbuf_tensor(p.un("tkge"), [128, 2048], F32) as getile,
        nc.sbuf_tensor(p.un("tkrbc"), [128, 2048], F32) as rbc,
        nc.psum_tensor(p.un("tkp1"), [128, 16], F32) as rb_ps,
        nc.psum_tensor(p.un("tkp2"), [128, 2048], F32) as rbc_ps,
        nc.psum_tensor(p.un("tkp3"), [128, 2], F32) as bc2_ps,
        nc.psum_tensor(p.un("tkp4"), [128, 1], F32) as sh_ps,
        nc.psum_tensor(p.un("tkp5"), [1, 1], F32) as nl_ps,
        nc.Block() as blk,
    ):
        b_tk, b_tm = tk.n, tm.n
        cand = csb[:, 0:1]
        cnt = csb[:, 1:2]
        b01 = csb[:, 2:3]
        delta = csb[:, 3:4]
        bc2sb = csb[:, 4:6]

        @blk.tensor
        def _(t):
            ins = None
            for c in range(16):
                ins = t.matmul(rb_ps[:, c:c + 1],
                               rr[:, c * 128:(c + 1) * 128],
                               p.t["ones_f"][0:1, 0:1], start=True, stop=True)
            for b in range(4):
                ins = t.matmul(rbc_ps[:, b * 512:(b + 1) * 512],
                               p.t["ones1_f"][:, :],
                               rr[:, b * 512:(b + 1) * 512],
                               start=True, stop=True)
            ins.then_inc(tm.h, 1)
            tm.n += 1
            for r_ in range(ROUNDS):
                t.wait_ge(tk.h, b_tk + 8 + r_ * 11)
                t.matmul(bc2_ps[:, :], p.t["ones1_f"][:, :], lohw,
                         start=True, stop=True).then_inc(tm.h, 1)
                tm.n += 1
                t.wait_ge(tk.h, b_tk + 13 + r_ * 11)
                t.matmul(sh_ps[:, :], p.t["sshift_sb"][:, :], b01,
                         start=True, stop=True).then_inc(tm.h, 1)
                tm.n += 1
                t.wait_ge(tk.h, b_tk + 14 + r_ * 11)
                t.matmul(nl_ps[:, :], cand, delta,
                         start=True, stop=True).then_inc(tm.h, 1)
                tm.n += 1

        @blk.vector
        def _(v):
            def stp(ins):
                ins.then_inc(tk.h, 1)
                tk.n += 1
                v.wait_ge(tk.h, tk.n)

            v.wait_ge(tm.h, b_tm + 1)
            stp(v.tensor_copy(rb[:, :], rb_ps[:, :]))
            stp(v.tensor_copy(rbc[:, :], rbc_ps[:, :]))
            stp(v.tensor_reduce(lo, rr[:, :], AX.X, ALU.min))
            stp(v.tensor_reduce(hi, rr[:, :], AX.X, ALU.max))
            stp(v.tensor_scalar(hi, hi, 0.001, None, ALU.add))
            for r_ in range(ROUNDS):
                stp(v.tensor_sub(wv, hi, lo))
                stp(v.tensor_scalar(wv, wv, 1.0 / 128.0, None, ALU.mult))
                stp(v.tensor_copy(lohw[:, 0:1], lo))
                v.wait_ge(tm.h, b_tm + 2 + 3 * r_)
                stp(v.tensor_copy(bc2sb, bc2_ps[:, :]))
                stp(v.scalar_tensor_tensor(cand, p.t["iota_sb"][:, :],
                                           bc2sb[:, 1:2], bc2sb[:, 0:1],
                                           ALU.mult, ALU.add))
                stp(v.tensor_scalar(getile[:, :], rbc[:, :], cand, None,
                                    ALU.is_ge))
                stp(v.reduce_sum(cnt, getile[:, :], AX.X))
                stp(v.tensor_scalar(b01, cnt, 1024.0, None, ALU.is_ge))
                v.wait_ge(tm.h, b_tm + 3 + 3 * r_)
                stp(v.tensor_sub(delta, b01, sh_ps[:, :]))
                v.wait_ge(tm.h, b_tm + 4 + 3 * r_)
                stp(v.tensor_copy(lo, nl_ps[0:1, :]))
                stp(v.tensor_add(hi, lo, wv))

        @blk.scalar
        def _(sc):
            pass


def _emit_gather(p, g, tr_d, pos_i, idx_i, w2sb, smalls, xg, xnt, BIGF):
    nc = p.nc
    tk = p.sem("tg_a")
    tm = p.sem("tg_m")
    te = p.sem("tg_e")
    ts_ = p.gsem("tg_s")
    tl = p.sem("tg_l")
    tgg = p.gsem("tg_g")
    ttp = p.sem("tg_tp")
    tev = p.sem("tg_ev")

    rr = None
    tkr = None

    with (
        nc.sbuf_tensor(p.un("tg1"), [128, 16], F32) as prsb,
        nc.sbuf_tensor(p.un("tg2"), [1, 64], F32) as csrow,
        nc.sbuf_tensor(p.un("tg3"), [128, 32], F32) as pairs,
        nc.sbuf_tensor(p.un("tg4"), [128, 32], F32) as trsb,
        nc.sbuf_tensor(p.un("tg5"), [128, 8], F32) as esb,
        nc.sbuf_tensor(p.un("tg6"), [128, 4], F32) as colsc,
        nc.sbuf_tensor(p.un("tgxt"), [128, 8 * 2048], BF16) as xg_tm,
        nc.psum_tensor(p.un("tgp1"), [128, 16], F32) as pr_ps,
        nc.psum_tensor(p.un("tgp2"), [1, 16], F32) as cs_ps,
        nc.psum_tensor(p.un("tgp3"), [128, 16], F32) as ex_ps,
        nc.psum_tensor(p.un("tgp4"), [1, 128], F32) as tm_ps,
        nc.psum_tensor(p.un("tgp5"), [128, 1], F32) as b1_ps,
        nc.psum_tensor(p.un("tgp6"), [128, 8 * 128], BF16) as tp_ps,
        nc.Block() as blk,
    ):
        b_tk, b_tm, b_te = tk.n, tm.n, te.n
        b_ts, b_tl, b_tg, b_tp, b_tev = ts_.n, tl.n, tgg.n, ttp.n, tev.n
        rb = p.t["rb_sb"]
        m_b = p.t["mb_sb"]
        pos_f = p.t["posf_sb"]
        tkr = p.t["tkr_sb"]
        tau_b = colsc[:, 0:1]
        cmax = colsc[:, 1:2]
        nbias = colsc[:, 2:3]
        recb2 = colsc[:, 3:4]

        @blk.tensor
        def _(t):
            t.matmul(b1_ps[:, :], p.t["ones1_f"][:, :], tkr[:, 0:1],
                     start=True, stop=True).then_inc(tm.h, 1)
            tm.n += 1
            t.wait_ge(tk.h, b_tk + 2)
            t.matmul(pr_ps[:, :], p.t["lstrict_sb"][:, :], m_b[:, :],
                     start=True, stop=True)
            ins = t.matmul(cs_ps[:, :], p.t["ones_f"][:, 0:1], m_b[:, :],
                           start=True, stop=True)
            ins.then_inc(tm.h, 1)
            tm.n += 1
            t.wait_ge(tk.h, b_tk + 13)
            t.matmul(ex_ps[:, :], p.t["ones1_f"][:, :], csrow[:, 32:48],
                     start=True, stop=True).then_inc(tm.h, 1)
            tm.n += 1
            t.wait_ge(tk.h, b_tk + 22)
            t.matmul(tm_ps[:, :], cmax, p.t["ident_f"][:, :],
                     start=True, stop=True).then_inc(tm.h, 1)
            tm.n += 1
            t.wait_ge(tk.h, b_tk + 23)
            t.matmul(b1_ps[:, :], p.t["ones1_f"][:, :], tkr[:, 8:9],
                     start=True, stop=True).then_inc(tm.h, 1)
            tm.n += 1
            t.wait_ge(te.h, b_te + 1)
            t.wait_ge(tk.h, b_tk + 25)
            t.matmul(tm_ps[:, :], cmax, p.t["ident_f"][:, :],
                     start=True, stop=True).then_inc(tm.h, 1)
            tm.n += 1
            t.wait_ge(tk.h, b_tk + 27)
            t.matmul(b1_ps[:, :], p.t["ones1_f"][:, :], tkr[:, 9:10],
                     start=True, stop=True).then_inc(tm.h, 1)
            tm.n += 1
            for dc in range(16):
                t.wait_ge(tgg.h, b_tg + 8 * 16)
                if dc >= 1:
                    t.wait_ge(tev.h, b_tev + dc)
                ins = None
                for c in range(8):
                    ins = t.transpose(
                        tp_ps[:, c * 128:(c + 1) * 128],
                        xg_tm[:, c * 2048 + dc * 128:c * 2048 + dc * 128 + 128],
                        p.t["ident_bf"][:, :])
                ins.then_inc(ttp.h, 1)
                ttp.n += 1

        @blk.vector
        def _(v):
            def stp(ins):
                ins.then_inc(tk.h, 1)
                tk.n += 1
                v.wait_ge(tk.h, tk.n)

            v.wait_ge(tm.h, b_tm + 1)
            stp(v.tensor_copy(tau_b, b1_ps[:, :]))
            stp(v.tensor_scalar(m_b[:, :], rb[:, :], tau_b, None,
                                ALU.is_ge))
            v.wait_ge(tm.h, b_tm + 2)
            stp(v.tensor_copy(prsb[:, :], pr_ps[:, :]))
            stp(v.tensor_copy(csrow[:, 0:16], cs_ps[0:1, :]))
            a, b = csrow[:, 0:16], csrow[:, 16:32]
            for s_ in (1, 2, 4, 8):
                stp(v.tensor_copy(b[:, 0:s_], a[:, 0:s_]))
                stp(v.tensor_add(b[:, s_:16], a[:, s_:16], a[:, 0:16 - s_]))
                a, b = b, a
            stp(v.tensor_sub(csrow[:, 32:48], a, cs_ps[0:1, :]))
            v.wait_ge(tm.h, b_tm + 3)
            stp(v.tensor_add(pos_f[:, :], prsb[:, :], ex_ps[:, :]))
            stp(v.tensor_scalar(pos_f[:, :], pos_f[:, :], -BIGF, None,
                                ALU.add))
            stp(v.tensor_mul(pos_f[:, :], pos_f[:, :], m_b[:, :]))
            stp(v.tensor_scalar(pos_f[:, :], pos_f[:, :], BIGF, None,
                                ALU.add))
            stp(v.tensor_copy(pos_i[:, :], pos_f[:, :]))
            stp(v.tensor_copy(
                pairs[:, :].rearrange("p (c two) -> p c two", two=2)[:, :, 0:1],
                p.t["tval_sb"][:, :].rearrange("p (c o) -> p c o", o=1)))
            stp(v.tensor_copy(
                pairs[:, :].rearrange("p (c two) -> p c two", two=2)[:, :, 1:2],
                rb[:, :].rearrange("p (c o) -> p c o", o=1)))
            v.wait_ge(tl.h, b_tl + 16 * 8)
            trv = trsb[:, :].rearrange("p (c two) -> p c two", two=2)
            stp(v.tensor_copy(idx_i[:, :],
                              trv[:, 0:8, 0:1].rearrange("p c o -> p (c o)")))
            stp(v.reduce_max(cmax, trv[:, 0:8, 1:2].rearrange(
                "p c o -> p (c o)"), AX.X))
            v.wait_ge(tm.h, b_tm + 4)
            stp(v.tensor_reduce(tkr[:, 8:9], tm_ps[0:1, :], AX.X, ALU.max,
                                negate=True))
            v.wait_ge(tm.h, b_tm + 5)
            stp(v.tensor_copy(nbias, b1_ps[:, :]))
            v.wait_ge(te.h, b_te + 1)
            stp(v.reduce_sum(cmax, esb[:, :], AX.X))
            v.wait_ge(tm.h, b_tm + 6)
            stp(v.tensor_reduce(tkr[:, 9:10], tm_ps[0:1, :], AX.X,
                                ALU.add))
            stp(v.reciprocal(tkr[:, 9:10], tkr[:, 9:10]))
            v.wait_ge(tm.h, b_tm + 7)
            stp(v.tensor_copy(recb2, b1_ps[:, :]))
            stp(v.tensor_scalar(w2sb[:, :], esb[:, :], recb2, None,
                                ALU.mult))
            for dc in range(16):
                v.wait_ge(ttp.h, b_tp + dc + 1)
                v.tensor_copy(xg[:, dc * 1024:(dc + 1) * 1024],
                              tp_ps[:, :]).then_inc(tev.h, 1)
                tev.n += 1

        @blk.scalar
        def _(sc):
            sc.wait_ge(tk.h, b_tk + 24)
            trv2 = trsb[:, :].rearrange("p (c two) -> p c two", two=2)
            sc.activation(esb[:, :],
                          trv2[:, 0:8, 1:2].rearrange("p c o -> p (c o)"),
                          AF.Exp, bias=nbias).then_inc(te.h, 1)
            te.n += 1

        @blk.gpsimd
        def _(gp):
            gp.wait_ge(tk.h, b_tk + 20)
            for c in range(16):
                gp.indirect_dma_start(
                    out=tr_d.ap(),
                    out_offset=IndirectOffsetOnAxis(pos_i[:, c:c + 1], 0),
                    in_=pairs[:, 2 * c:2 * c + 2], in_offset=None,
                    bounds_check=p.t["bcr1023"], oob_is_err=False,
                ).then_inc(ts_.h, 16)
                ts_.n += 16
                gp.wait_ge(ts_.h, ts_.n)
            gp.wait_ge(tk.h, b_tk + 21)
            for c in range(8):
                gp.indirect_dma_start(
                    out=xg_tm[:, c * 2048:(c + 1) * 2048], out_offset=None,
                    in_=xnt.ap(),
                    in_offset=IndirectOffsetOnAxis(idx_i[:, c:c + 1], 0),
                    bounds_check=p.t["bcr2047"], oob_is_err=False,
                ).then_inc(tgg.h, 16)
                tgg.n += 16
                gp.wait_ge(tgg.h, tgg.n)

        @blk.sync
        def _(s):
            s.wait_ge(ts_.h, b_ts + 16 * 16)
            for c in range(8):
                s.dma_start(out=trsb[:, 2 * c:2 * c + 2],
                            in_=tr_d.ap()[c * 128:(c + 1) * 128, :]
                            ).then_inc(tl.h, 16)
                tl.n += 16


def _emit_maskprep(p, g, msk, smalls):
    nc = p.nc
    lm = p.gsem("mp_l")
    mv = p.sem("mp_v")
    mt = p.sem("mp_t")
    with (
        nc.psum_tensor(p.un("mpp"), [1, 128], F32) as t_ps,
        nc.Block() as blk,
    ):
        b_lm, b_mv, b_mt = lm.n, mv.n, mt.n
        mmax = smalls[:, 0:1]
        tmpc = smalls[:, 1:2]

        @blk.gpsimd
        def _(gp):
            src = g["l1_mT_d"].ap().rearrange("(c p) f -> p c f", p=128)
            dst = msk[:, :].rearrange("p (c f) -> p c f", c=8)
            gp.dma_start(out=dst, in_=src).then_inc(lm.h, 16)
            lm.n += 16

        @blk.vector
        def _(v):
            vch = p.sem("mp_vch")

            def stp(ins):
                ins.then_inc(vch.h, 1)
                vch.n += 1
                v.wait_ge(vch.h, vch.n)

            v.wait_ge(lm.h, b_lm + 16)
            stp(v.tensor_scalar(msk[:, :], msk[:, :],
                                float(np.sqrt(128.0)), None, ALU.mult))
            stp(v.tensor_reduce(tmpc, msk[:, :], AX.X, ALU.max))
            v.nop().then_inc(mv.h, 1)
            mv.n += 1
            v.wait_ge(mt.h, b_mt + 1)
            v.tensor_reduce(mmax[0:1, 0:1], t_ps[0:1, :], AX.X,
                            ALU.max).then_inc(mv.h, 1)
            mv.n += 1

        @blk.tensor
        def _(t):
            t.wait_ge(mv.h, b_mv + 1)
            t.matmul(t_ps[:, :], tmpc, p.t["ident_f"][:, :],
                     start=True, stop=True).then_inc(mt.h, 1)
            mt.n += 1

    return smalls[0:1, 0:1]


def _emit_yout(p, g, xg, w2sb, yrow, S):
    nc = p.nc
    yt = p.sem("yo_t")
    ye = p.sem("yo_e")
    yst = [p.sem("yo_s0"), p.sem("yo_s1")]
    with (
        nc.sbuf_tensor(p.un("yor"), [128, 2 * 2048], F32) as rowb,
        nc.psum_tensor(p.un("yop"), [128, 2 * 512], F32) as tp_ps,
        nc.Block() as blk,
    ):
        b_yt, b_ye = yt.n, ye.n
        b_ys = [s.n for s in yst]

        @blk.tensor
        def _(t):
            for c in range(8):
                for dc in range(16):
                    i = c * 16 + dc
                    j = i % 4
                    if i >= 1:
                        t.wait_ge(ye.h, b_ye + i)
                    ins = t.transpose(
                        tp_ps[:, j * 128:(j + 1) * 128],
                        xg[:, dc * S + c * 128:dc * S + c * 128 + 128],
                        p.t["ident_f"][:, :])
                    ins.then_inc(yt.h, 1)
                    yt.n += 1

        @blk.vector
        def _(v):
            for c in range(8):
                for dc in range(16):
                    i = c * 16 + dc
                    j = i % 4
                    v.wait_ge(yt.h, b_yt + i + 1)
                    if dc == 0 and c >= 2:
                        v.wait_ge(yst[c % 2].h, b_ys[c % 2] + 16 * (c // 2))
                    v.tensor_scalar(
                        rowb[:, (c % 2) * 2048 + dc * 128:
                             (c % 2) * 2048 + dc * 128 + 128],
                        tp_ps[:, j * 128:(j + 1) * 128],
                        w2sb[:, c:c + 1], None, ALU.mult).then_inc(ye.h, 1)
                    ye.n += 1

        @blk.sync
        def _(s):
            for c in range(8):
                s.wait_ge(ye.h, b_ye + (c + 1) * 16)
                ins = s.dma_start(
                    out=yrow.ap()[c * 128:(c + 1) * 128, :],
                    in_=rowb[:, (c % 2) * 2048:(c % 2) * 2048 + 2048])
                ins.then_inc(yst[c % 2].h, 16)
                yst[c % 2].n += 16
            s.wait_ge(yst[0].h, yst[0].n)
            s.wait_ge(yst[1].h, yst[1].n)


def _emit_finaladd(p, g, xd, xnd, yrow, pos_i):
    nc = p.nc
    fl = [p.sem("fa_l0"), p.sem("fa_l1")]
    fn = [p.sem("fa_n0"), p.sem("fa_n1")]
    fz = p.sem("fa_z")
    fg = p.gsem("fa_g")
    ft = p.sem("fa_t")
    fe = p.sem("fa_e")
    fs = [p.sem("fa_s0"), p.sem("fa_s1")]

    with (
        nc.sbuf_tensor(p.un("fax"), [128, 2 * 2048], F32) as xsb,
        nc.sbuf_tensor(p.un("faxn"), [128, 2 * 2048], BF16) as xnsb,
        nc.sbuf_tensor(p.un("fau"), [128, 2 * 2048], F32) as usb,
        nc.psum_tensor(p.un("fap"), [128, 4 * 128], F32) as tp_ps,
        nc.Block() as blk,
    ):
        b_fl = [s.n for s in fl]
        b_fn = [s.n for s in fn]
        b_fz, b_fg, b_ft, b_fe = fz.n, fg.n, ft.n, fe.n
        b_fs = [s.n for s in fs]

        @blk.sync
        def _(s):
            for tc in range(16):
                j = tc % 2
                if tc >= 2:
                    s.wait_ge(fs[j].h, b_fs[j] + 16 * (tc // 2))
                src = xd.ap()[:, tc * 128:(tc + 1) * 128].rearrange(
                    "(c p) f -> p c f", p=128)
                dst = xsb[:, j * 2048:(j + 1) * 2048].rearrange(
                    "p (c f) -> p c f", c=16)
                s.dma_start(out=dst, in_=src).then_inc(fl[j].h, 16)
                fl[j].n += 16
                src = xnd.ap()[:, tc * 128:(tc + 1) * 128].rearrange(
                    "(c p) f -> p c f", p=128)
                dst = xnsb[:, j * 2048:(j + 1) * 2048].rearrange(
                    "p (c f) -> p c f", c=16)
                s.dma_start(out=dst, in_=src).then_inc(fn[j].h, 16)
                fn[j].n += 16

        @blk.gpsimd
        def _(gp):
            for tc in range(16):
                j = tc % 2
                if tc >= 2:
                    gp.wait_ge(ft.h, b_ft + 16 * (tc - 1))
                gp.memset(usb[:, j * 2048:(j + 1) * 2048],
                          0.0).then_inc(fz.h, 1)
                fz.n += 1
                gp.wait_ge(fz.h, fz.n)
                gp.indirect_dma_start(
                    out=usb[:, j * 2048:(j + 1) * 2048], out_offset=None,
                    in_=yrow.ap(),
                    in_offset=IndirectOffsetOnAxis(pos_i[:, tc:tc + 1], 0),
                    bounds_check=p.t["bcr1023"], oob_is_err=False,
                ).then_inc(fg.h, 16)
                fg.n += 16
                gp.wait_ge(fg.h, fg.n)
            gp.wait_ge(fs[0].h, fs[0].n + 8 * 16)
            gp.wait_ge(fs[1].h, fs[1].n + 8 * 16)

        @blk.tensor
        def _(t):
            for tc in range(16):
                j = tc % 2
                t.wait_ge(fg.h, b_fg + 16 * (tc + 1))
                for dc in range(16):
                    i = tc * 16 + dc
                    jj = i % 4
                    if i >= 1:
                        t.wait_ge(fe.h, b_fe + i)
                    ins = t.transpose(
                        tp_ps[:, jj * 128:(jj + 1) * 128],
                        usb[:, j * 2048 + dc * 128:j * 2048 + dc * 128 + 128],
                        p.t["ident_f"][:, :])
                    ins.then_inc(ft.h, 1)
                    ft.n += 1

        @blk.vector
        def _(v):
            vch = p.sem("fa_vch")
            for tc in range(16):
                j = tc % 2
                v.wait_ge(fl[j].h, b_fl[j] + 16 * (tc // 2 + 1))
                v.wait_ge(fn[j].h, b_fn[j] + 16 * (tc // 2 + 1))
                if tc >= 2:
                    v.wait_ge(fs[j].h, b_fs[j] + 16 * (tc // 2))
                for dc in range(16):
                    i = tc * 16 + dc
                    jj = i % 4
                    v.wait_ge(ft.h, b_ft + i + 1)
                    xs_ = xsb[:, j * 2048 + dc * 128:j * 2048 + dc * 128 + 128]
                    xns = xnsb[:, j * 2048 + dc * 128:j * 2048 + dc * 128 + 128]
                    v.scalar_tensor_tensor(
                        xs_, xns, 0.5, xs_, ALU.mult,
                        ALU.add).then_inc(vch.h, 1)
                    vch.n += 1
                    v.wait_ge(vch.h, vch.n)
                    v.scalar_tensor_tensor(
                        xs_, tp_ps[:, jj * 128:(jj + 1) * 128], 0.5, xs_,
                        ALU.mult, ALU.add).then_inc(fe.h, 1)
                    fe.n += 1

        @blk.scalar
        def _(sc):
            for tc in range(16):
                j = tc % 2
                sc.wait_ge(fe.h, b_fe + 16 * (tc + 1))
                dst = xd.ap()[:, tc * 128:(tc + 1) * 128].rearrange(
                    "(c p) f -> p c f", p=128)
                src = xsb[:, j * 2048:(j + 1) * 2048].rearrange(
                    "p (c f) -> p c f", c=16)
                sc.dma_start(out=dst, in_=src).then_inc(fs[j].h, 16)
                fs[j].n += 16


def emit_l1attn(p, xd, g):
    nc = p.nc
    S = 1024
    SCALE = float(1.0 / np.sqrt(128.0))
    qd, kd, attnd = p.t["qd"], p.t["kd"], p.t["attnd"]
    vd2 = p.t["vd2"]

    for th in range(2):
        t0 = th * S
        with nc.sbuf_tensor(p.un("laxn"), [128, 16 * S], BF16) as xn:
            emit_ln(p, xd, xn, S, t0)
            emit_proj(p, g, xn, S, t0, g["l1a_wq_d"], out_dram=qd)
            emit_proj(p, g, xn, S, t0, g["l1a_wk_d"], out_dram=kd)
            emit_proj(p, g, xn, S, t0, g["l1a_wv_d"], vtile=vd2)

    wt_map = {}
    for (qb, kc), v in p.t["l1a_tiles"].items():
        if v is None or v == "full":
            wt_map[(qb, kc)] = v
        else:
            wt_map[(qb, kc)] = p.t["l1a_masks_sb"][v]
    emit_attn_sT(p, g, qd, kd, vd2, attnd, scale=SCALE, use_norm_shift=False,
                 softcap=30.0, win_tiles=wt_map, Tq=T, Tk=T)

    for th in range(2):
        emit_phaseC(p, g, attnd, g["l1a_wo_d"], xd, th, bias_sb=None, S=S)


def emit_l0attn(p, xd, g):
    nc = p.nc
    S = 1024
    SCALE = float(2048.0 ** (-0.5))
    qd, kd, attnd = p.t["qd"], p.t["kd"], p.t["attnd"]
    vd2 = p.t["vd2"]
    qn_d, kn_d = p.t["qn_d"], p.t["kn_d"]

    for th in range(2):
        t0 = th * S
        with nc.sbuf_tensor(p.un("aaxn"), [128, 16 * S], BF16) as xn:
            emit_ln(p, xd, xn, S, t0)
            emit_proj(p, g, xn, S, t0, g["l0_wq_d"], out_dram=qd,
                      bias_sb=g["bq_sb"], norm_rows=qn_d)
            emit_proj(p, g, xn, S, t0, g["l0_wk_d"], out_dram=kd,
                      bias_sb=g["bk_sb"], norm_rows=kn_d)
            emit_proj(p, g, xn, S, t0, g["l0_wv_d"], vtile=vd2,
                      bias_sb=g["bv_sb"])

    emit_attn_sT(p, g, qd, kd, vd2, attnd, scale=SCALE, use_norm_shift=True,
                 qn_d=qn_d, kn_d=kn_d, Tq=T, Tk=T)

    for th in range(2):
        emit_phaseC(p, g, attnd, g["l0_wo_d"], xd, th, bias_sb=g["bo_sb"],
                    S=S)


def emit_attn_sT(p, g, qd, kd, vd2, attnd, scale, use_norm_shift,
                 softcap=None, win_tiles=None, Tq=2048, Tk=2048,
                 qn_d=None, kn_d=None, rope=False, extra_neg=None):
    nc = p.nc
    NKC = Tk // 128
    NQB = Tq // 512

    def tiles_for(qb):
        out = []
        for kc in range(NKC):
            wt_ = win_tiles[(qb, kc)] if win_tiles is not None else "full"
            if wt_ is None:
                continue
            out.append((kc, None if wt_ == "full" else wt_))
        return out

    sched = [(h, qb, tiles_for(qb)) for h in range(NH) for qb in range(NQB)]

    lq = p.sem("at_lq")
    lk = p.sem("at_lk")
    lv = p.sem("at_lv")
    lr = p.sem("at_lr")
    smm = p.sem("at_smm")
    pex = p.sem("at_pex")
    avm = p.sem("at_avm")
    dnv = p.sem("at_dnv")
    bcm = p.sem("at_bcm")
    bev = p.sem("at_bev")
    aev = p.sem("at_aev")
    ast = [p.gsem("at_ast0"), p.gsem("at_ast1")]
    hrw = p.sem("at_hrw")
    hbc = p.sem("at_hbc")
    hbv = p.sem("at_hbv")

    rpd = p.sem("at_rpd")
    head_tiles = None

    from contextlib import ExitStack
    with ExitStack() as _stk:
        def _sb(nm, shape, dt_):
            return _stk.enter_context(nc.sbuf_tensor(p.un(nm), shape, dt_))
        def _ps(nm, shape, dt_):
            return _stk.enter_context(nc.psum_tensor(p.un(nm), shape, dt_))
        qsb = _sb("atq", [128, Tq], BF16)
        ksb = _sb("atk", [128, Tk], BF16)
        qro = _sb("atqr", [128, Tq if rope else 1], BF16)
        kro = _sb("atkr", [128, Tk if rope else 1], BF16)
        qsw = _sb("atqs", [128, Tq if rope else 1], BF16)
        ksw = _sb("atks", [128, Tk if rope else 1], BF16)
        rtmp = _sb("atrt", [128, Tq if rope else 1], BF16)
        vsb = _sb("atv", [128, NKC * 128], BF16)
        psb = _sb("atp", [128, 2 * 512], BF16)
        tansb = _sb("atth", [128, 512], F32)
        rowsc = _sb("atr", [1, 64], F32)
        qknsb = _sb("atqk", [1, 2 * 2048], BF16)
        biassc = _sb("atb", [128, 1], F32)
        recrow = _sb("atrr", [1, 512], F32)
        recb = _sb("atrb", [128, 512], F32)
        aosb = _sb("atao", [128, 2 * 512], BF16)
        sps = _ps("atps", [128, 2 * 512], F32)
        aps = _ps("atpa", [128, 512], F32)
        dps = _ps("atpd", [1, 512], F32)
        bps = _ps("atpb", [128, 512], F32)
        cps = _ps("atpc", [128, 1], F32)
        blk = _stk.enter_context(nc.Block())
        b_lq, b_lk, b_lv, b_lr = lq.n, lk.n, lv.n, lr.n
        b_smm, b_pex, b_avm = smm.n, pex.n, avm.n
        b_dnv, b_bcm, b_bev, b_aev = dnv.n, bcm.n, bev.n, aev.n
        b_ast = [s.n for s in ast]
        b_hrw, b_hbc, b_hbv = hrw.n, hbc.n, hbv.n
        b_rpd = rpd.n
        qrow = qknsb[:, 0:2048]
        krow = qknsb[:, 2048:4096]
        tiny = rowsc[:, 0:64]
        ntile = [len(tl) for (_, _, tl) in sched]
        cum = [0]
        for n_ in ntile:
            cum.append(cum[-1] + n_)

        @blk.sync
        def _(s):
            for h in range(NH):
                if h >= 1:
                    s.wait_ge(aev.h, b_aev + h * NQB)
                s.dma_start(out=qsb[:],
                            in_=qd.ap()[h * 128:(h + 1) * 128, 0:Tq]
                            ).then_inc(lq.h, 16)
                lq.n += 16
                if rope:
                    s.dma_start(out=qsw[0:64, :],
                                in_=qd.ap()[h * 128 + 64:h * 128 + 128, 0:Tq]
                                ).then_inc(lq.h, 16)
                    s.dma_start(out=qsw[64:128, :],
                                in_=qd.ap()[h * 128:h * 128 + 64, 0:Tq]
                                ).then_inc(lq.h, 16)
                    lq.n += 32
                s.dma_start(out=ksb[:],
                            in_=kd.ap()[h * 128:(h + 1) * 128, 0:Tk]
                            ).then_inc(lk.h, 16)
                lk.n += 16
                if rope:
                    s.dma_start(out=ksw[0:64, :],
                                in_=kd.ap()[h * 128 + 64:h * 128 + 128, 0:Tk]
                                ).then_inc(lk.h, 16)
                    s.dma_start(out=ksw[64:128, :],
                                in_=kd.ap()[h * 128:h * 128 + 64, 0:Tk]
                                ).then_inc(lk.h, 16)
                    lk.n += 32
                src = vd2.ap()[h, 0:NKC].rearrange("c p hd -> p c hd")
                dst = vsb[:, :].rearrange("p (c hd) -> p c hd", c=NKC)
                s.dma_start(out=dst, in_=src).then_inc(lv.h, 16)
                lv.n += 16
                if use_norm_shift:
                    s.dma_start(out=qrow[:, 0:Tq],
                                in_=qn_d.ap()[h:h + 1, 0:Tq]).then_inc(lr.h, 16)
                    lr.n += 16
                    s.dma_start(out=krow[:, 0:Tk],
                                in_=kn_d.ap()[h:h + 1, 0:Tk]).then_inc(lr.h, 16)
                    lr.n += 16

        @blk.gpsimd
        def _(gp):
            for i, (h, qb, tl) in enumerate(sched):
                j = i % 2
                gp.wait_ge(aev.h, b_aev + i + 1)
                t0 = qb * 512
                ins = gp.dma_start(
                    out=attnd.ap()[h * 128:(h + 1) * 128, t0:t0 + 512],
                    in_=aosb[:, j * 512:(j + 1) * 512])
                ins.then_inc(ast[j].h, 16)
                ast[j].n += 16
            gp.wait_ge(ast[0].h, ast[0].n)
            gp.wait_ge(ast[1].h, ast[1].n)

        @blk.vector
        def _(v):
            if not use_norm_shift:
                v.memset(biassc[:, :], -float(softcap)).then_inc(hbv.h, 1)
                hbv.n += 1
            vch = p.sem("at_vch")

            def stp(ins):
                ins.then_inc(vch.h, 1)
                vch.n += 1
                v.wait_ge(vch.h, vch.n)

            for i, (h, qb, tl) in enumerate(sched):
                if rope and qb == 0:
                    v.wait_ge(lq.h, b_lq + 48 * (h + 1))
                    v.wait_ge(lk.h, b_lk + 48 * (h + 1))
                    if h >= 1:
                        v.wait_ge(smm.h, b_smm + cum[i])
                    cfs, sfs = p.t["cf_sb"], p.t["sf_sb"]
                    for sq_, sw_, dst, TT in ((qsb, qsw, qro, Tq),
                                              (ksb, ksw, kro, Tk)):
                        stp(v.tensor_mul(rtmp[:, 0:TT], sq_[:, 0:TT],
                                         cfs[:, 0:TT]))
                        stp(v.tensor_mul(dst[:, 0:TT], sw_[:, 0:TT],
                                         sfs[:, 0:TT]))
                        stp(v.tensor_add(dst[:, 0:TT], dst[:, 0:TT],
                                         rtmp[:, 0:TT]))
                    v.nop().then_inc(rpd.h, 1)
                    rpd.n += 1
                if use_norm_shift and qb == 0:
                    v.wait_ge(lr.h, b_lr + 32 * (h + 1))
                    if h >= 1:
                        v.wait_ge(pex.h, b_pex + cum[i])
                    stp(v.tensor_reduce(tiny[:, 0:1], qrow[:, 0:Tq], AX.X,
                                        ALU.max))
                    stp(v.tensor_reduce(tiny[:, 1:2], krow[:, 0:Tk], AX.X,
                                        ALU.max))
                    stp(v.tensor_mul(tiny[:, 2:3], tiny[:, 0:1], tiny[:, 1:2]))
                    if extra_neg is not None:
                        stp(v.tensor_scalar(tiny[:, 2:3], tiny[:, 2:3],
                                            extra_neg, None, ALU.add))
                    v.tensor_scalar(tiny[:, 3:4], tiny[:, 2:3], -scale, None,
                                    ALU.mult).then_inc(hrw.h, 1)
                    hrw.n += 1
                    v.wait_ge(hbc.h, b_hbc + h + 1)
                    v.tensor_copy(biassc[:, :], cps[:, :]).then_inc(hbv.h, 1)
                    hbv.n += 1
                v.wait_ge(avm.h, b_avm + cum[i + 1])
                v.reciprocal(recrow[:, :], dps[0:1, :]).then_inc(dnv.h, 1)
                dnv.n += 1
                v.wait_ge(bcm.h, b_bcm + i + 1)
                if i >= 1:
                    v.wait_ge(aev.h, b_aev + i)
                v.tensor_copy(recb[:, :], bps[:, :]).then_inc(bev.h, 1)
                bev.n += 1
                v.wait_ge(bev.h, bev.n)
                j = i % 2
                if i >= 2:
                    v.wait_ge(ast[j].h, b_ast[j] + 16 * (i // 2))
                v.tensor_mul(aosb[:, j * 512:(j + 1) * 512], aps[:, :],
                             recb[:, :]).then_inc(aev.h, 1)
                aev.n += 1

        @blk.scalar
        def _(sc):
            thx = p.sem("at_thx")
            for i, (h, qb, tl) in enumerate(sched):
                if qb == 0:
                    sc.wait_ge(hbv.h, b_hbv + (h + 1 if use_norm_shift else 1))
                for u, (kc, mt) in enumerate(tl):
                    idx = cum[i] + u
                    jj = idx % 2
                    sc.wait_ge(smm.h, b_smm + idx + 1)
                    if idx >= 2:
                        sc.wait_ge(avm.h, b_avm + idx - 1)
                    src = sps[:, jj * 512:(jj + 1) * 512]
                    if softcap is not None:
                        sc.activation(tansb[:, :], src, AF.Tanh,
                                      scale=scale / softcap).then_inc(thx.h, 1)
                        thx.n += 1
                        sc.wait_ge(thx.h, thx.n)
                        src = tansb[:, :]
                        sc2 = float(softcap)
                    else:
                        sc2 = scale
                    sc.activation(psb[:, jj * 512:(jj + 1) * 512], src,
                                  AF.Exp, bias=biassc[:, 0:1],
                                  scale=sc2).then_inc(pex.h, 1)
                    pex.n += 1

        @blk.tensor
        def _(t):
            for i, (h, qb, tl) in enumerate(sched):
                t0 = qb * 512
                if use_norm_shift and qb == 0:
                    t.wait_ge(hrw.h, b_hrw + h + 1)
                    if h >= 1:
                        t.wait_ge(hbv.h, b_hbv + h)
                    t.matmul(cps[:, :], p.t["ones1_f"][:, :], tiny[:, 3:4],
                             start=True, stop=True).then_inc(hbc.h, 1)
                    hbc.n += 1
                if rope:
                    t.wait_ge(rpd.h, b_rpd + h + 1)
                else:
                    t.wait_ge(lq.h, b_lq + 16 * (h + 1))
                    t.wait_ge(lk.h, b_lk + 16 * (h + 1))
                t.wait_ge(lv.h, b_lv + 16 * (h + 1))
                if i >= 1:
                    t.wait_ge(aev.h, b_aev + i)
                    t.wait_ge(dnv.h, b_dnv + i)
                qs_ = qro if rope else qsb
                ks_ = kro if rope else ksb
                for u, (kc, mt) in enumerate(tl):
                    idx = cum[i] + u
                    jj = idx % 2
                    if idx >= 2:
                        t.wait_ge(pex.h, b_pex + idx - 1)
                    ins = t.matmul(sps[:, jj * 512:(jj + 1) * 512],
                                   ks_[:, kc * 128:(kc + 1) * 128],
                                   qs_[:, t0:t0 + 512],
                                   start=True, stop=(mt is None))
                    if mt is not None:
                        ins = t.matmul(sps[:, jj * 512:(jj + 1) * 512],
                                       p.t["ident_bf"][:, :], mt,
                                       start=False, stop=True)
                    ins.then_inc(smm.h, 1)
                    smm.n += 1
                    t.wait_ge(pex.h, b_pex + idx + 1)
                    t.matmul(aps[:, :], vsb[:, kc * 128:(kc + 1) * 128],
                             psb[:, jj * 512:(jj + 1) * 512],
                             start=(u == 0), stop=(u == len(tl) - 1))
                    ins = t.matmul(dps[0:1, :], p.t["ones_bf"][:, 0:1],
                                   psb[:, jj * 512:(jj + 1) * 512],
                                   start=(u == 0), stop=(u == len(tl) - 1))
                    ins.then_inc(avm.h, 1)
                    avm.n += 1
                t.wait_ge(dnv.h, b_dnv + i + 1)
                if i >= 1:
                    t.wait_ge(bev.h, b_bev + i)
                t.matmul(bps[:, :], p.t["ones1_f"][:, :], recrow[:, :],
                         start=True, stop=True).then_inc(bcm.h, 1)
                bcm.n += 1


def emit_phaseC(p, g, attnd, wo_dram, xd, th, bias_sb=None, S=1024,
                res_scale=None, resid_sb=None):
    nc = p.nc
    t0 = th * S
    nb = S // 512
    NOC, NKC = 16, 16
    la = p.sem("pc_la")
    lw = [p.gsem("pc_lw0"), p.gsem("pc_lw1")]
    lxo = [p.sem("pc_lxo0"), p.sem("pc_lxo1")]
    mm = p.sem("pc_mm")
    ev = p.sem("pc_ev")
    st = [p.sem("pc_st0"), p.sem("pc_st1")]

    with (
        nc.sbuf_tensor(p.un("pca"), [128, 16 * S], BF16) as asb,
        nc.sbuf_tensor(p.un("pcw"), [128, 2 * 2048], BF16) as wt,
        nc.sbuf_tensor(p.un("pcxo"), [128, 2 * S], F32) as xold,
        nc.sbuf_tensor(p.un("pcxn"), [128, 2 * S], F32) as xnew,
        nc.psum_tensor(p.un("pcp"), [128, 2 * S], F32) as pps,
        nc.Block() as blk,
    ):
        b_la = la.n
        b_lw = [s.n for s in lw]
        b_lxo = [s.n for s in lxo]
        b_mm, b_ev = mm.n, ev.n
        b_st = [s.n for s in st]

        @blk.sync
        def _(s):
            src = attnd.ap()[:, t0:t0 + S].rearrange("(c p) f -> p c f", p=128)
            dst = asb[:, :].rearrange("p (c f) -> p c f", c=16)
            s.dma_start(out=dst, in_=src).then_inc(la.h, 16)
            la.n += 16
            if resid_sb is not None:
                return
            for oc in range(NOC):
                j = oc % 2
                if oc >= 2:
                    s.wait_ge(ev.h, b_ev + oc - 1)
                s.dma_start(out=xold[:, j * S:(j + 1) * S],
                            in_=xd.ap()[oc * 128:(oc + 1) * 128, t0:t0 + S]
                            ).then_inc(lxo[j].h, 16)
                lxo[j].n += 16

        @blk.gpsimd
        def _(gp):
            for oc in range(NOC):
                j = oc % 2
                if oc >= 2:
                    gp.wait_ge(mm.h, b_mm + oc - 1)
                src = wo_dram.ap()[:, oc * 128:(oc + 1) * 128]
                src = src.rearrange("(c p) h -> p c h", p=128)
                dst = wt[:, j * 2048:(j + 1) * 2048].rearrange(
                    "p (c h) -> p c h", c=16)
                gp.dma_start(out=dst, in_=src).then_inc(lw[j].h, 16)
                lw[j].n += 16

        @blk.tensor
        def _(t):
            t.wait_ge(la.h, b_la + 16)
            for oc in range(NOC):
                j = oc % 2
                t.wait_ge(lw[j].h, b_lw[j] + 16 * (oc // 2 + 1))
                if oc >= 2:
                    t.wait_ge(ev.h, b_ev + oc - 1)
                ins = None
                for kc in range(NKC):
                    for b in range(nb):
                        ins = t.matmul(
                            pps[:, j * S + b * 512:j * S + (b + 1) * 512],
                            wt[:, j * 2048 + kc * 128:j * 2048 + kc * 128 + 128],
                            asb[:, kc * S + b * 512:kc * S + (b + 1) * 512],
                            start=(kc == 0), stop=(kc == NKC - 1))
                ins.then_inc(mm.h, 1)
                mm.n += 1

        @blk.vector
        def _(v):
            for oc in range(NOC):
                j = oc % 2
                v.wait_ge(mm.h, b_mm + oc + 1)
                if resid_sb is not None:
                    dst = resid_sb[:, oc * S:(oc + 1) * S]
                    v.tensor_add(dst, pps[:, j * S:(j + 1) * S],
                                 dst).then_inc(ev.h, 1)
                    ev.n += 1
                    continue
                v.wait_ge(lxo[j].h, b_lxo[j] + 16 * (oc // 2 + 1))
                if oc >= 2:
                    v.wait_ge(st[j].h, b_st[j] + 16 * (oc // 2))
                if bias_sb is not None:
                    ins = v.scalar_tensor_tensor(
                        xnew[:, j * S:(j + 1) * S], pps[:, j * S:(j + 1) * S],
                        bias_sb[:, oc:oc + 1],
                        xold[:, j * S:(j + 1) * S], ALU.add, ALU.add)
                elif res_scale is not None:
                    ins = v.scalar_tensor_tensor(
                        xnew[:, j * S:(j + 1) * S], pps[:, j * S:(j + 1) * S],
                        float(res_scale), xold[:, j * S:(j + 1) * S],
                        ALU.mult, ALU.add)
                else:
                    ins = v.tensor_add(xnew[:, j * S:(j + 1) * S],
                                       pps[:, j * S:(j + 1) * S],
                                       xold[:, j * S:(j + 1) * S])
                ins.then_inc(ev.h, 1)
                ev.n += 1

        @blk.scalar
        def _(sc):
            if resid_sb is not None:
                return
            for oc in range(NOC):
                j = oc % 2
                sc.wait_ge(ev.h, b_ev + oc + 1)
                ins = sc.dma_start(
                    out=xd.ap()[oc * 128:(oc + 1) * 128, t0:t0 + S],
                    in_=xnew[:, j * S:(j + 1) * S])
                ins.then_inc(st[j].h, 16)
                st[j].n += 16
            sc.wait_ge(st[0].h, st[0].n)
            sc.wait_ge(st[1].h, st[1].n)



def build(nsub=N_SUB):
    nc = bass.Bass("TRN2", target_bir_lowering=False)
    p = Prog(nc)
    g = {}

    xin = nc.dram_tensor("xin", [D, T], F32, kind="ExternalInput")
    xout = nc.dram_tensor("xout", [D, T], F32, kind="ExternalOutput")
    g["l0_wg_d"] = nc.dram_tensor("l0_wg", [D, H0], I32, kind="ExternalInput")
    g["l0_wu_d"] = nc.dram_tensor("l0_wu", [D, H0], I32, kind="ExternalInput")
    g["l0_wd_d"] = nc.dram_tensor("l0_wd", [H0, D], I32, kind="ExternalInput")
    g["l0_sg_d"] = nc.dram_tensor("l0_sg", [128, 44], F32, kind="ExternalInput")
    g["l0_su_d"] = nc.dram_tensor("l0_su", [128, 44], F32, kind="ExternalInput")
    g["l0_sd_d"] = nc.dram_tensor("l0_sd", [128, 16], F32, kind="ExternalInput")
    for nm in ("l0_wq", "l0_wk", "l0_wv", "l0_wo",
               "l1a_wq", "l1a_wk", "l1a_wv", "l1a_wo"):
        g[nm + "_d"] = nc.dram_tensor(nm, [D, D], F32, kind="ExternalInput")
    for nm in ("l0_bq", "l0_bk", "l0_bv", "l0_bo"):
        g[nm + "_d"] = nc.dram_tensor(nm, [128, 16], F32, kind="ExternalInput")

    for nm in ("l1_wq", "l1_wk", "l1_wv", "l1_wo"):
        g[nm + "_d"] = nc.dram_tensor(nm, [D, D], F32, kind="ExternalInput")
    g["l1_w1_d"] = nc.dram_tensor("l1_w1", [D, F1], F32, kind="ExternalInput")
    g["l1_w3_d"] = nc.dram_tensor("l1_w3", [D, F1], F32, kind="ExternalInput")
    g["l1_w2_d"] = nc.dram_tensor("l1_w2", [F1, D], F32, kind="ExternalInput")
    g["l1_mT_d"] = nc.dram_tensor("l1_maskT", [1024, 1024], F32,
                                  kind="ExternalInput")
    g["l1_cos_d"] = nc.dram_tensor("l1_cos", [64, 1024], F32,
                                   kind="ExternalInput")
    g["l1_sin_d"] = nc.dram_tensor("l1_sin", [64, 1024], F32,
                                   kind="ExternalInput")
    g["l1_wr_d"] = nc.dram_tensor("l1_wr", [128, 16], F32,
                                  kind="ExternalInput")
    g["l1_g1_d"] = nc.dram_tensor("l1_g1", [128, 16], F32,
                                  kind="ExternalInput")
    g["l1_g2_d"] = nc.dram_tensor("l1_g2", [128, 16], F32,
                                  kind="ExternalInput")

    xd = nc.dram_tensor("xd", [D, T], F32, kind="Internal")
    p.t["xnd"] = nc.dram_tensor("xnd", [D, T], BF16, kind="Internal")
    p.t["xnt"] = nc.dram_tensor("xnt", [T, D], BF16, kind="Internal")
    p.t["tr_d"] = nc.dram_tensor("tr_d", [1024, 2], F32, kind="Internal")
    p.t["yrow"] = nc.dram_tensor("yrow", [1024, D], F32, kind="Internal")
    p.t["qd"] = nc.dram_tensor("qd", [D, T], BF16, kind="Internal")
    p.t["kd"] = nc.dram_tensor("kd", [D, T], BF16, kind="Internal")
    p.t["attnd"] = nc.dram_tensor("attnd", [D, T], BF16, kind="Internal")
    p.t["vd2"] = nc.dram_tensor("vd2", [16, 16, 128, 128], BF16,
                                kind="Internal")
    p.t["qn_d"] = nc.dram_tensor("qn_d", [16, T], BF16, kind="Internal")
    p.t["kn_d"] = nc.dram_tensor("kn_d", [16, T], BF16, kind="Internal")

    cst = _consts()
    p.t["bcr1023"] = nc.alloc_register(mybir.EngineType.Pool, "bcr1023")
    p.t["bcr2047"] = nc.alloc_register(mybir.EngineType.Pool, "bcr2047")

    from contextlib import ExitStack
    with ExitStack() as _stk:
        def _sb(name, shape, dt_):
            return _stk.enter_context(nc.sbuf_tensor(name, shape, dt_))
        ones_f = _sb("c_ones_f", [128, 1], F32)
        ones1_f = _sb("c_ones1_f", [1, 128], F32)
        ones_bf = _sb("c_ones_bf", [128, 1], BF16)
        ones1_bf = _sb("c_ones1_bf", [1, 128], BF16)
        ident_bf = _sb("c_ident_bf", [128, 128], BF16)
        ident_f = _sb("c_ident_f", [128, 128], F32)
        sg_sb = _sb("c_sg", [128, 44], F32)
        su_sb = _sb("c_su", [128, 44], F32)
        sdh_sb = _sb("c_sdh", [128, 16], F32)
        bq_sb = _sb("c_bq", [128, 16], F32)
        bk_sb = _sb("c_bk", [128, 16], F32)
        bv_sb = _sb("c_bv", [128, 16], F32)
        bo_sb = _sb("c_bo", [128, 16], F32)
        wmask_sb = _sb("c_wmask", [128, 8 * 512], BF16)
        lstrict_sb = _sb("c_lstrict", [128, 128], F32)
        sshift_sb = _sb("c_sshift", [128, 128], F32)
        iota_sb = _sb("c_iota", [128, 1], F32)
        tval_sb = _sb("c_tval", [128, 16], F32)
        wr_bf = _sb("c_wrbf", [128, 16], BF16)
        cf_sb = _sb("c_cf", [128, 1024], BF16)
        sf_sb = _sb("c_sf", [128, 1024], BF16)
        g1_sb = _sb("c_g1", [128, 16], F32)
        g2_sb = _sb("c_g2", [128, 16], F32)
        p.t["ones_f"] = ones_f
        p.t["ones1_f"] = ones1_f
        p.t["ones_bf"] = ones_bf
        p.t["ones1_bf"] = ones1_bf
        p.t["ident_bf"] = ident_bf
        p.t["ident_f"] = ident_f
        g["sg_sb"] = sg_sb
        g["su_sb"] = su_sb
        g["sdh_sb"] = sdh_sb
        g["bq_sb"] = bq_sb
        g["bk_sb"] = bk_sb
        g["bv_sb"] = bv_sb
        g["bo_sb"] = bo_sb
        g["g1_sb"] = g1_sb
        g["g2_sb"] = g2_sb
        p.t["lstrict_sb"] = lstrict_sb
        p.t["sshift_sb"] = sshift_sb
        p.t["iota_sb"] = iota_sb
        p.t["tval_sb"] = tval_sb
        p.t["wr_bf"] = wr_bf
        p.t["cf_sb"] = cf_sb
        p.t["sf_sb"] = sf_sb

        tiles_map, masks = _l1attn_tiles()
        p.t["l1a_tiles"] = tiles_map
        deltas = sorted(masks.keys())
        mask_cat = np.concatenate([masks[dl] for dl in deltas], axis=1)
        p.t["l1a_masks_sb"] = {
            dl: wmask_sb[:, i * 512:(i + 1) * 512]
            for i, dl in enumerate(deltas)}

        ini = p.sem("ini")
        inisd = p.sem("inisd")
        cd = {
            "ones_f": nc.inline_tensor(cst["ones_f"], "k_ones_f"),
            "ones1_f": nc.inline_tensor(cst["ones1_f"], "k_ones1_f"),
            "ones_bf": nc.inline_tensor(cst["ones_bf"], "k_ones_bf"),
            "ones1_bf": nc.inline_tensor(cst["ones1_bf"], "k_ones1_bf"),
            "ident_bf": nc.inline_tensor(cst["ident_bf"], "k_ident_bf"),
            "ident_f": nc.inline_tensor(cst["ident_f"], "k_ident_f"),
            "lstrict_sb": nc.inline_tensor(cst["lstrict"], "k_lstrict"),
            "sshift_sb": nc.inline_tensor(cst["sshift"], "k_sshift"),
            "iota_sb": nc.inline_tensor(cst["iota"], "k_iota"),
            "tval_sb": nc.inline_tensor(cst["tval"], "k_tval"),
        }
        wmask_d = nc.inline_tensor(mask_cat, "k_wmask")
        with nc.Block() as blk:
            @blk.sync
            def _(s):
                for name, dram in cd.items():
                    s.dma_start(out=p.t[name][:, :],
                                in_=dram.ap()).then_inc(ini.h, 16)
                    ini.n += 16
                s.dma_start(out=sg_sb[:, :],
                            in_=g["l0_sg_d"].ap()).then_inc(ini.h, 16)
                ini.n += 16
                s.dma_start(out=su_sb[:, :],
                            in_=g["l0_su_d"].ap()).then_inc(ini.h, 16)
                ini.n += 16
                s.dma_start(out=sdh_sb[:, :],
                            in_=g["l0_sd_d"].ap()).then_inc(inisd.h, 16)
                inisd.n += 16
                for nm, sb in (("l0_bq", bq_sb), ("l0_bk", bk_sb),
                               ("l0_bv", bv_sb), ("l0_bo", bo_sb)):
                    s.dma_start(out=sb[:, :],
                                in_=g[nm + "_d"].ap()).then_inc(ini.h, 16)
                    ini.n += 16
                s.dma_start(out=wmask_sb[:, :],
                            in_=wmask_d.ap()).then_inc(ini.h, 16)
                ini.n += 16
                s.dma_start(out=g1_sb[:, :],
                            in_=g["l1_g1_d"].ap()).then_inc(ini.h, 16)
                ini.n += 16
                s.dma_start(out=g2_sb[:, :],
                            in_=g["l1_g2_d"].ap()).then_inc(ini.h, 16)
                ini.n += 16
                s.dma_start(out=xd.ap(), in_=xin.ap()).then_inc(ini.h, 16)
                ini.n += 16
                s.wait_ge(ini.h, ini.n)

            @blk.gpsimd
            def _(gp):
                gp.reg_mov(p.t["bcr1023"], 1023)
                gp.reg_mov(p.t["bcr2047"], 2047)
                inig = p.gsem("ini_g")
                for dst_, src_ in ((wr_bf[:, :], g["l1_wr_d"]),
                                   (cf_sb[0:64, :], g["l1_cos_d"]),
                                   (cf_sb[64:128, :], g["l1_cos_d"]),
                                   (sf_sb[0:64, :], g["l1_sin_d"]),
                                   (sf_sb[64:128, :], g["l1_sin_d"])):
                    gp.dma_start(out=dst_,
                                 in_=src_.ap()).then_inc(inig.h, 16)
                    inig.n += 16
                    gp.wait_ge(inig.h, inig.n)
                gp.wait_ge(ini.h, ini.n)
                gp.nop().then_inc(ini.h, 1)
                ini.n += 1

            @blk.vector
            def _(v):
                v.wait_ge(inisd.h, inisd.n)
                v.tensor_scalar(sdh_sb[:, :], sdh_sb[:, :], 0.5, None,
                                ALU.mult)
                inig2 = p.gsem("ini_g")
                v.wait_ge(inig2.h, inig2.n)
                v.tensor_scalar(sf_sb[0:64, :], sf_sb[0:64, :], -1.0, None,
                                ALU.mult)

        subs = nsub if isinstance(nsub, (list, tuple)) else \
            list(range(1, nsub + 1))
        for s_id in subs:
            if s_id in (1, 3):
                for th in range(2):
                    emit_l0ffn_half(p, xd, th, g)
            elif s_id == 2:
                emit_l0attn(p, xd, g)
            elif s_id in (4, 6):
                emit_l1ffn(p, xd, g)
            elif s_id == 5:
                emit_l1attn(p, xd, g)

        fin = p.sem("fin")
        with nc.Block() as blk:
            @blk.sync
            def _(s):
                s.dma_start(out=xout.ap(), in_=xd.ap()).then_inc(fin.h, 16)
                fin.n += 16
                s.wait_ge(fin.h, fin.n)

    return nc


def _in_map(gw, x):
    m = {
        "xin": x,
        "l0_wg": gw["l0_wg"],
        "l0_wu": gw["l0_wu"],
        "l0_wd": gw["l0_wd"],
        "l0_sg": gw["l0_sg"],
        "l0_su": gw["l0_su"],
        "l0_sd": gw["l0_sd"],
    }
    for nm in ("l0_wq", "l0_wk", "l0_wv", "l0_wo",
               "l0_bq", "l0_bk", "l0_bv", "l0_bo",
               "l1a_wq", "l1a_wk", "l1a_wv", "l1a_wo",
               "l1_wq", "l1_wk", "l1_wv", "l1_wo",
               "l1_w1", "l1_w3", "l1_w2",
               "l1_maskT", "l1_cos", "l1_sin",
               "l1_wr", "l1_g1", "l1_g2"):
        m[nm] = gw[nm]
    return m


def kernel(**inputs):
    gw, xs = host_layout(inputs)
    nc = build(N_SUB)
    in_maps = [_in_map(gw, xs[b]) for b in range(NB)]
    res = run_bass_kernel_spmd(nc, in_maps, core_ids=list(range(NB)))
    out = np.stack([np.ascontiguousarray(res.results[b]["xout"].T)
                    for b in range(NB)])
    return out
